# version 09
# baseline (speedup 1.0000x reference)
import contextlib
import numpy as np
import concourse.bass as bass
import concourse.mybir as mybir
from concourse.bass_utils import run_bass_kernel_spmd

F32 = mybir.dt.float32
BF16 = mybir.dt.bfloat16
AF = mybir.ActivationFunctionType
ALU = mybir.AluOpType
AX = mybir.AxisListType

S = 2048
E = 1024
C = 8
NH = 16
HD = 64
DC = 256
DR = 128
EKR = E - DR
NE = 8
SLAB = 512
NSLAB = S // SLAB
NT = S // 128
EPS = 1e-5

_STAGES = ("ffn1", "attn", "ffn2")
_DBG = []
_DBG_SHAPES = {}


class Sem:
    def __init__(self, h):
        self.h = h
        self.n = 0

    def inc(self, k=1):
        self.n += k
        return self.n


class Em:

    ENGS = ("T", "V", "S", "G", "Y")

    def __init__(self):
        self.q = {e: [] for e in self.ENGS}
        self.waited = {e: {} for e in self.ENGS}

    def op(self, e, fn):
        self.q[e].append(fn)

    def wait(self, e, sem, val):
        if val <= 0:
            return
        w = self.waited[e]
        if w.get(id(sem), 0) >= val:
            return
        w[id(sem)] = val
        self.q[e].append(lambda eng, s=sem, v=val: eng.wait_ge(s.h, v))


def build(nc, stages=_STAGES):
    def din(name, shape):
        return nc.dram_tensor(name, list(shape), F32, kind="ExternalInput").ap()

    xT = din("xT", (E, S))
    wq_d = din("wq", (E, DC)); bq_d = din("bq", (DC, 1))
    wqc_d = din("wqc", (DC, EKR)); bqc_d = din("bqc", (EKR, 1))
    wqr_d = din("wqr", (E, DR)); bqr_d = din("bqr", (DR, 1))
    wkv_d = din("wkv", (E, DC)); bkv_d = din("bkv", (DC, 1))
    wkr_d = din("wkr", (E, DR)); bkr_d = din("bkr", (DR, 1))
    wkc_d = din("wkc", (DC, EKR)); bkc_d = din("bkc", (EKR, 1))
    wv_d = din("wv", (DC, E)); bv_d = din("bv", (1, E))
    wo_d = din("wo", (E, E)); bo_d = din("bo", (E, 1))
    rw_d = din("rwT", (E, NE)); rb_d = din("rb", (1, NE))
    ew_d = din("ewT", (NE, E, E))
    eb_d = din("eb", (NE, E))

    yT = nc.dram_tensor("yT", [E, S], F32, kind="ExternalOutput").ap()
    w_scr = nc.dram_tensor("w_scr", [NE * C * 128, E], BF16, kind="Internal").ap()

    import ml_dtypes
    bfdt = ml_dtypes.bfloat16
    ident_bf_d = nc.inline_tensor(np.eye(128, dtype=bfdt), "ident_bf_dr").ap()
    ident_f_d = nc.inline_tensor(np.eye(128, dtype=np.float32), "ident_f32_dr").ap()
    onesc_bf_d = nc.inline_tensor(np.ones((128, 1), dtype=bfdt), "onesc_bf_dr").ap()
    onesc_f_d = nc.inline_tensor(np.ones((128, 1), dtype=np.float32), "onesc_f_dr").ap()
    onesr_f_d = nc.inline_tensor(np.ones((1, 128), dtype=np.float32), "onesr_f_dr").ap()
    halfr_f_d = nc.inline_tensor(np.full((1, 128), 0.5, dtype=np.float32), "halfr_f_dr").ap()
    onesr_bf_d = nc.inline_tensor(np.ones((1, 128), dtype=bfdt), "onesr_bf_dr").ap()
    tri_d = nc.inline_tensor(np.triu(np.ones((128, 128), dtype=bfdt)), "tri_bf_dr").ap()
    bo2 = np.zeros((128, 2), dtype=bfdt)
    bo2[0:64, 0] = 1
    bo2[64:128, 1] = 1
    bo2_d = nc.inline_tensor(bo2, "bo2_bf_dr").ap()
    esel = np.zeros((NE, NE * 128), dtype=bfdt)
    for _e in range(NE):
        esel[_e, _e * 128:(_e + 1) * 128] = 1
    esel_d = nc.inline_tensor(esel, "esel_bf_dr").ap()
    eps_d = nc.inline_tensor(np.full((1, 1), EPS, dtype=np.float32), "eps_dr").ap()
    inv_freq = 1.0 / 10000.0 ** (np.arange(0, DR, 2, dtype=np.float64) / DR)
    t_pos = np.arange(S, dtype=np.float64)
    fr = np.concatenate([inv_freq, inv_freq])[:, None] * t_pos[None, :]
    cosT_np = np.cos(fr).astype(np.float32)
    srot_np = np.sin(fr).astype(np.float32)
    srot_np[0:64, :] = -srot_np[0:64, :]
    cosT_d = nc.inline_tensor(cosT_np, "cosT_dr").ap()
    srot_d = nc.inline_tensor(srot_np, "srotT_dr").ap()

    em = Em()

    with contextlib.ExitStack() as top:
        ec = top.enter_context
        sY = Sem(ec(nc.semaphore("sY")))
        sG = Sem(ec(nc.semaphore("sG")))
        sT = Sem(ec(nc.semaphore("sT")))
        sV = Sem(ec(nc.semaphore("sV")))
        sS = Sem(ec(nc.semaphore("sS")))
        sB = Sem(ec(nc.semaphore("sB")))

        def y_dma(dst, src):
            v = sY.inc(16)
            em.op("Y", lambda e, d=dst, s=src: e.dma_start(out=d, in_=s).then_inc(sY.h, 16))
            return v

        def g_dma(dst, src):
            v = sG.inc(16)
            em.op("G", lambda e, d=dst, s=src: e.dma_start(out=d, in_=s).then_inc(sG.h, 16))
            return v

        def mark(e, sem):
            v = sem.inc()
            prev = em.q[e][-1]

            def wrapped(eng, p=prev, s=sem):
                ins = p(eng)
                ins.then_inc(s.h, 1)
                return ins

            em.q[e][-1] = wrapped
            return v

        def barrier():
            for e in ("T", "V", "S"):
                em.op(e, lambda eng: eng.drain().then_inc(sB.h, 1))
                sB.inc()
            for e, dsem in (("G", sG), ("Y", sY)):
                em.wait(e, dsem, dsem.n)
                em.op(e, lambda eng: eng.sem_inc(sB.h, 1))
                sB.inc()
            tgt = sB.n
            for e in Em.ENGS:
                em.wait(e, sB, tgt)

        def dump(name, ap):
            if name not in _DBG:
                return
            shp = list(ap.shape)
            _DBG_SHAPES[name] = shp
            d = nc.dram_tensor("dbg_" + name, shp, ap.dtype, kind="ExternalOutput").ap()
            for sem in (sV, sS, sT):
                em.wait("G", sem, sem.n)
            gv = g_dma(d, ap)
            for e in ("T", "V", "S", "Y"):
                em.wait(e, sG, gv)

        x_res = ec(nc.sbuf_tensor("x_res", [128, C * S], F32))
        A = ec(nc.sbuf_tensor("A", [128, C * S], BF16))
        cq_sb = ec(nc.sbuf_tensor("cq_sb", [128, 2 * S], BF16))
        ckv_sb = ec(nc.sbuf_tensor("ckv_sb", [128, 2 * S], BF16))
        qrg_sb = ec(nc.sbuf_tensor("qrg_sb", [128, S], BF16))
        krg_sb = ec(nc.sbuf_tensor("krg_sb", [128, S], BF16))
        wqc_bf = ec(nc.sbuf_tensor("wqc_bf", [128, 2 * EKR], BF16))
        wkc_bf = ec(nc.sbuf_tensor("wkc_bf", [128, 2 * EKR], BF16))
        wv_bf = ec(nc.sbuf_tensor("wv_bf", [128, 2 * E], BF16))
        Wbuf = [ec(nc.sbuf_tensor(f"Wbuf{i}", [128, E], BF16)) for i in range(6)]
        Wcast = [ec(nc.sbuf_tensor(f"Wcast{i}", [128, 2 * E], BF16)) for i in range(2)]
        bq_c = ec(nc.sbuf_tensor("bq_c", [128, 2], F32))
        bqc_c = ec(nc.sbuf_tensor("bqc_c", [128, 7], F32))
        bqr_c = ec(nc.sbuf_tensor("bqr_c", [128, 1], F32))
        bkv_c = ec(nc.sbuf_tensor("bkv_c", [128, 2], F32))
        bkr_c = ec(nc.sbuf_tensor("bkr_c", [128, 1], F32))
        bkc_c = ec(nc.sbuf_tensor("bkc_c", [128, 7], F32))
        bo_c = ec(nc.sbuf_tensor("bo_c", [128, 8], F32))
        bv_row = ec(nc.sbuf_tensor("bv_row", [1, E], BF16))
        rw_f = ec(nc.sbuf_tensor("rw_f", [128, C * NE], F32))
        rb_row = ec(nc.sbuf_tensor("rb_row", [1, NE], F32))
        eb_bf = ec(nc.sbuf_tensor("eb_bf", [NE, E], BF16))
        identbf_sb = ec(nc.sbuf_tensor("identbf_sb", [128, 128], BF16))
        identf_sb = ec(nc.sbuf_tensor("identf_sb", [128, 128], F32))
        onesc_bf = ec(nc.sbuf_tensor("onesc_bf", [128, 1], BF16))
        onesc_f = ec(nc.sbuf_tensor("onesc_f", [128, 1], F32))
        onesr_f = ec(nc.sbuf_tensor("onesr_f", [1, 128], F32))
        halfr_f = ec(nc.sbuf_tensor("halfr_f", [1, 128], F32))
        onesr_bf = ec(nc.sbuf_tensor("onesr_bf", [1, 128], BF16))
        tri_sb = ec(nc.sbuf_tensor("tri_sb", [128, 128], BF16))
        bo2_sb = ec(nc.sbuf_tensor("bo2_sb", [128, 2], BF16))
        esel_sb = ec(nc.sbuf_tensor("esel_sb", [NE, NE * 128], BF16))
        rows = ec(nc.sbuf_tensor("rows", [1, 6 * SLAB], F32))
        nb_row = ec(nc.sbuf_tensor("nb_row", [1, 2], F32))
        eps_sb = ec(nc.sbuf_tensor("eps_sb", [1, 1], F32))
        negB = [ec(nc.sbuf_tensor(f"negB{p}", [128, 1], F32)) for p in range(2)]

        r_mean = rows[:, 0 * SLAB:1 * SLAB]
        r_msq = rows[:, 1 * SLAB:2 * SLAB]
        r_var = rows[:, 2 * SLAB:3 * SLAB]
        r_rstd = rows[:, 3 * SLAB:4 * SLAB]
        r_b = rows[:, 4 * SLAB:5 * SLAB]
        r_hw = rows[:, 5 * SLAB:6 * SLAB]

        def r3(ap_):
            return ap_.rearrange("(c p) n -> p c n", p=128)

        ld_x = y_dma(x_res[:].rearrange("p (c t) -> p c t", c=C), r3(xT))
        for col, src in ((bq_c, bq_d), (bqc_c, bqc_d), (bkv_c, bkv_d),
                         (bkc_c, bkc_d), (bo_c, bo_d)):
            n = src.shape[0] // 128
            for cc in range(n):
                y_dma(col[:, cc:cc + 1], src[cc * 128:(cc + 1) * 128, 0:1])
        y_dma(bqr_c[:], bqr_d)
        y_dma(bkr_c[:], bkr_d)
        y_dma(rb_row[:], rb_d)
        y_dma(identbf_sb[:], ident_bf_d)
        y_dma(identf_sb[:], ident_f_d)
        y_dma(onesc_bf[:], onesc_bf_d)
        y_dma(onesc_f[:], onesc_f_d)
        y_dma(onesr_f[:], onesr_f_d)
        y_dma(halfr_f[:], halfr_f_d)
        y_dma(onesr_bf[:], onesr_bf_d)
        y_dma(tri_sb[:], tri_d)
        y_dma(bo2_sb[:], bo2_d)
        y_dma(esel_sb[:], esel_d)
        y_dma(eps_sb[:], eps_d)
        ld_consts = sY.n

        g_dma(wqc_bf[:].rearrange("p (c n) -> p c n", c=2), r3(wqc_d))
        g_dma(wkc_bf[:].rearrange("p (c n) -> p c n", c=2), r3(wkc_d))
        g_dma(wv_bf[:].rearrange("p (c n) -> p c n", c=2), r3(wv_d))
        g_dma(bv_row[:], bv_d)
        ld_rw = y_dma(rw_f[:].rearrange("p (c n) -> p c n", c=C), r3(rw_d))
        g_dma(eb_bf[:], eb_d)
        ld_small_g = sG.n

        ew_r = ew_d.rearrange("e (k p) n -> e k p n", p=128)
        scr_r = w_scr.rearrange("(i p) n -> i p n", p=128)
        cast_done = [0] * (NE * C)
        cast_g = [0] * (NE * C)
        if "ffn1" not in stages:
            wbuf_free = [0, 0, 0, 0]
            for i in range(NE * C):
                e_i, k_i = divmod(i, C)
                par = i % 4
                em.wait("G", sY, wbuf_free[par])
                gv = g_dma(Wcast[par][:], ew_r[e_i, k_i])
                em.wait("Y", sG, gv)
                cast_done[i] = y_dma(scr_r[i], Wcast[par][:])
                wbuf_free[par] = cast_done[i]

        def x_chunk(c, t0, tl):
            return x_res[:, c * S + t0: c * S + t0 + tl]

        ffn_bank_free = [None]

        def emit_ln_sq(t0, xbf_buf, sq_buf):
            for c in range(C):
                em.op("V", lambda e, c=c: e.tensor_copy(
                    xbf_buf[:, c * SLAB:(c + 1) * SLAB], x_chunk(c, t0, SLAB)))
                em.op("V", lambda e, c=c: e.tensor_mul(
                    sq_buf[:, c * SLAB:(c + 1) * SLAB],
                    xbf_buf[:, c * SLAB:(c + 1) * SLAB],
                    xbf_buf[:, c * SLAB:(c + 1) * SLAB]))
            return mark("V", sV)

        def emit_ln(t0, dst_fn, psA, psB, ps_s0, ps_s1, xbf_buf, sq_buf, tmp_buf,
                    v_sq=None, defer_apply=False, inplace_sq=False):
            em.wait("T", sS, sS.n)
            if inplace_sq:
                for c in range(C):
                    em.op("V", lambda e, c=c: e.tensor_copy(
                        xbf_buf[:, c * SLAB:(c + 1) * SLAB], x_chunk(c, t0, SLAB)))
                v_cast = mark("V", sV)
                em.wait("T", sV, v_cast)
                em.wait("T", sY, ld_consts)
                for c in range(C):
                    em.op("T", lambda e, c=c: e.matmul(
                        ps_s0, onesc_bf[:], xbf_buf[:, c * SLAB:(c + 1) * SLAB],
                        start=(c == 0), stop=(c == C - 1), skip_group_check=True))
                t_r0 = mark("T", sT)
                em.wait("V", sT, t_r0)
                for c in range(C):
                    em.op("V", lambda e, c=c: e.tensor_mul(
                        xbf_buf[:, c * SLAB:(c + 1) * SLAB],
                        xbf_buf[:, c * SLAB:(c + 1) * SLAB],
                        xbf_buf[:, c * SLAB:(c + 1) * SLAB]))
                v_sq2 = mark("V", sV)
                em.wait("T", sV, v_sq2)
                for c in range(C):
                    em.op("T", lambda e, c=c: e.matmul(
                        ps_s1, onesc_bf[:], xbf_buf[:, c * SLAB:(c + 1) * SLAB],
                        start=(c == 0), stop=(c == C - 1), skip_group_check=True))
                t_red = mark("T", sT)
            else:
                if v_sq is None:
                    v_sq = emit_ln_sq(t0, xbf_buf, sq_buf)
                em.wait("T", sV, v_sq)
                em.wait("T", sY, ld_consts)
                for c in range(C):
                    em.op("T", lambda e, c=c: e.matmul(
                        ps_s0, onesc_bf[:], xbf_buf[:, c * SLAB:(c + 1) * SLAB],
                        start=(c == 0), stop=(c == C - 1), skip_group_check=True))
                for c in range(C):
                    em.op("T", lambda e, c=c: e.matmul(
                        ps_s1, onesc_bf[:], sq_buf[:, c * SLAB:(c + 1) * SLAB],
                        start=(c == 0), stop=(c == C - 1), skip_group_check=True))
                t_red = mark("T", sT)
            if ffn_bank_free[0]:
                em.wait("T", sV, ffn_bank_free[0][3])
            em.wait("S", sT, t_red)
            em.op("S", lambda e: e.activation(r_mean, ps_s0, AF.Copy, scale=1.0 / E))
            s_mean = mark("S", sS)
            em.wait("V", sS, s_mean)
            em.op("V", lambda e: e.tensor_mul(r_msq, r_mean, r_mean))
            em.op("V", lambda e: e.scalar_tensor_tensor(
                r_var, ps_s1, 1.0 / E, r_msq, op0=ALU.mult, op1=ALU.subtract))
            v_var = mark("V", sV)
            em.wait("S", sV, v_var)
            em.op("S", lambda e: e.activation(r_rstd, r_var, AF.Ln, bias=eps_sb[0:1, 0:1]))
            em.op("S", lambda e: e.activation(r_rstd, r_rstd, AF.Exp, scale=-0.5))
            s_rstd = mark("S", sS)
            em.wait("V", sS, s_rstd)
            em.op("V", lambda e: e.scalar_tensor_tensor(
                r_b, r_mean, -1.0, r_rstd, op0=ALU.mult, op1=ALU.mult))
            v_b = mark("V", sV)
            em.wait("T", sS, s_rstd)
            em.op("T", lambda e: e.matmul(psA, onesr_f[:], r_rstd, start=True, stop=True,
                                          skip_group_check=True))
            em.wait("T", sV, v_b)
            em.op("T", lambda e: e.matmul(psB, onesr_f[:], r_b, start=True, stop=True,
                                          skip_group_check=True))
            t_bc = mark("T", sT)

            def apply():
                em.wait("V", sT, t_bc)
                for c in range(C):
                    em.op("V", lambda e, c=c: e.tensor_mul(
                        tmp_buf[:], x_chunk(c, t0, SLAB), psA))
                    em.op("V", lambda e, c=c: e.tensor_add(dst_fn(c), tmp_buf[:], psB))
                return mark("V", sV)

            if defer_apply:
                return apply
            return apply()

        def emit_ffn(store=False, tag="", cast=False):
            with contextlib.ExitStack() as fes:
                fc = fes.enter_context
                ps_mm = [fc(nc.psum_tensor(f"ps{tag}_mm{m}", [128, SLAB], F32)) for m in range(8)]
                xbf_buf = A[:, 0:C * SLAB]
                sq_buf = A[:, C * SLAB:2 * C * SLAB]
                tmp_buf = fc(nc.sbuf_tensor(f"f_tmp{tag}", [128, SLAB], F32))
                xln_buf = fc(nc.sbuf_tensor(f"f_xln{tag}", [128, C * SLAB], BF16))
                xe_buf = [fc(nc.sbuf_tensor(f"f_xe{tag}{i}", [128, SLAB], BF16)) for i in range(2)]
                maskT_sb = fc(nc.sbuf_tensor(f"f_maskT{tag}", [NE, SLAB], BF16))
                mask_f = fc(nc.sbuf_tensor(f"f_maskf{tag}", [128, NE], F32))
                mbc_sb = fc(nc.sbuf_tensor(f"f_mbc{tag}", [128, NE * SLAB], BF16))
                hwbc_sb = fc(nc.sbuf_tensor(f"f_hwbc{tag}", [128, SLAB], F32))
                e_sb = fc(nc.sbuf_tensor(f"f_e{tag}", [128, NE], F32))
                cols = fc(nc.sbuf_tensor(f"f_cols{tag}", [128, 8], F32))
                rsb_sb = fc(nc.sbuf_tensor(f"f_rsb{tag}", [128, NE], F32))
                rbb_sb = fc(nc.sbuf_tensor(f"f_rbb{tag}", [128, NE], F32))
                lg_sb = fc(nc.sbuf_tensor(f"f_lg{tag}", [128, NE], F32))
                lgf_sb = fc(nc.sbuf_tensor(f"f_lgf{tag}", [128, NE], F32))

                em.wait("T", sY, ld_consts)
                em.wait("T", sY, ld_rw)
                for c in range(C):
                    em.op("T", lambda e, c=c: e.matmul(
                        ps_mm[0][0:1, 0:NE], onesc_f[:], rw_f[:, c * NE:(c + 1) * NE],
                        start=(c == 0), stop=(c == C - 1), skip_group_check=True))
                t_rs = mark("T", sT)
                em.wait("V", sT, t_rs)
                em.op("V", lambda e: e.tensor_copy(rows[:, 0:NE], ps_mm[0][0:1, 0:NE]))
                v_rs = mark("V", sV)
                em.wait("T", sV, v_rs)
                em.op("T", lambda e: e.matmul(ps_mm[0][:, 0:NE], onesr_f[:], rows[:, 0:NE],
                                              start=True, stop=True, skip_group_check=True))
                em.op("T", lambda e: e.matmul(ps_mm[1][:, 0:NE], onesr_f[:], rb_row[:],
                                              start=True, stop=True, skip_group_check=True))
                t_rsb = mark("T", sT)
                em.wait("V", sT, t_rsb)
                em.op("V", lambda e: e.tensor_copy(rsb_sb[:], ps_mm[0][:, 0:NE]))
                em.op("V", lambda e: e.tensor_copy(rbb_sb[:], ps_mm[1][:, 0:NE]))
                v_rsb = mark("V", sV)
                em.wait("T", sV, v_rsb)
                wbuf_guard = [0] * 6
                v_sq_next = None
                ffn_bank_free[0] = None
                for s in range(NSLAB):
                    t0 = s * SLAB
                    apply_fn = emit_ln(
                        t0, lambda c: xln_buf[:, c * SLAB:(c + 1) * SLAB],
                        ps_mm[2][:], ps_mm[3][:],
                        ps_mm[0][0:1, :], ps_mm[1][0:1, :],
                        xbf_buf, sq_buf, tmp_buf, v_sq=v_sq_next, defer_apply=True)
                    dump(f"xln_s{s}", xln_buf[:])
                    dump(f"xbf_s{s}", xbf_buf[:])
                    em.op("V", lambda e: e.tensor_scalar_mul(r_msq, r_rstd, -1.0))
                    v_nr = mark("V", sV)
                    tile_cons = [0, 0, 0, 0]
                    tile_trc = [0, 0, 0, 0]
                    tile_ttr = [0, 0, 0, 0]
                    if ffn_bank_free[0]:
                        em.wait("T", sV, ffn_bank_free[0][5])
                    for tt in range(4):
                        psl = ps_mm[4 + (tt % 2)][:, 0:NE]
                        pst = ps_mm[6][:, 0:4]
                        if tt >= 2:
                            em.wait("T", sV, tile_cons[tt - 2])
                        for c in range(C):
                            em.op("T", lambda e, c=c, tt=tt, psl=psl, t0=t0: e.matmul(
                                psl, x_chunk(c, t0 + tt * 128, 128),
                                rw_f[:, c * NE:(c + 1) * NE],
                                start=(c == 0), stop=(c == C - 1), skip_group_check=True))
                        if tt == 0 and f"psl_pre_s{s}" in _DBG:
                            mark("T", sT)
                            em.wait("V", sT, sT.n)
                            em.op("V", lambda e, psl=psl: e.tensor_copy(lgf_sb[:], psl))
                            mark("V", sV)
                            dump(f"psl_pre_s{s}", lgf_sb[:])
                        em.wait("T", sV, v_nr)
                        em.op("T", lambda e, tt=tt, pst=pst: e.transpose(
                            pst[:, 0:1], r_mean[0:1, tt * 128:(tt + 1) * 128],
                            identf_sb[0:1, 0:1]))
                        em.op("T", lambda e, tt=tt, pst=pst: e.transpose(
                            pst[:, 1:2], r_msq[0:1, tt * 128:(tt + 1) * 128],
                            identf_sb[0:1, 0:1]))
                        t_lg = mark("T", sT)
                        if tt == 0 and f"psl_s{s}" in _DBG:
                            em.wait("V", sT, t_lg)
                            em.op("V", lambda e, psl=psl: e.tensor_copy(lg_sb[:], psl))
                            mark("V", sV)
                            dump(f"psl_s{s}", lg_sb[:])
                            dump(f"xck_s{s}", x_res[:, 0:128])
                            dump(f"rwf_s{s}", rw_f[:])
                        em.wait("S", sT, t_lg)
                        em.op("S", lambda e, pst=pst: e.copy(cols[:, 6:8], pst[:, 0:2]))
                        s_mc = mark("S", sS)
                        em.wait("V", sS, s_mc)
                        if tt >= 1:
                            em.wait("V", sT, tile_ttr[tt - 1])
                        em.op("V", lambda e, psl=psl: e.scalar_tensor_tensor(
                            lg_sb[:], rsb_sb[:], cols[:, 6:7], psl,
                            op0=ALU.mult, op1=ALU.subtract))
                        em.op("V", lambda e: e.scalar_tensor_tensor(
                            lgf_sb[:], lg_sb[:], cols[:, 7:8], rbb_sb[:],
                            op0=ALU.mult, op1=ALU.add))
                        v_lgf = mark("V", sV)
                        em.wait("S", sV, v_lgf)
                        em.op("S", lambda e: e.copy(lg_sb[:], lgf_sb[:]))
                        s_lgf = mark("S", sS)
                        em.wait("V", sS, s_lgf)
                        em.op("V", lambda e: e.reduce_max(
                            cols[:, 0:1], lg_sb[:], axis=AX.X, negate=True))
                        em.op("V", lambda e: e.reduce_max(cols[:, 1:2], lg_sb[:], axis=AX.X))
                        v_mx = mark("V", sV)
                        em.wait("S", sV, v_mx)
                        em.op("S", lambda e: e.activation(
                            e_sb[:], lg_sb[:], AF.Exp, bias=cols[:, 0:1],
                            accum_out=cols[:, 3:4]))
                        em.op("S", lambda e: e.copy(cols[:, 2:3], cols[:, 1:2]))
                        s_exp = mark("S", sS)
                        em.wait("V", sS, s_exp)
                        em.op("V", lambda e: e.reciprocal(cols[:, 4:5], cols[:, 3:4]))
                        em.op("V", lambda e: e.tensor_single_scalar(
                            mask_f[:], lg_sb[:], cols[:, 2:3], op=ALU.is_ge))
                        v_msk = mark("V", sV)
                        tile_cons[tt] = v_msk
                        if tt == 0:
                            dump(f"esb_s{s}", e_sb[:])
                            dump(f"cols_s{s}", cols[:])
                            dump(f"lgf_s{s}", lgf_sb[:])
                            dump(f"lg_s{s}", lg_sb[:])
                            dump(f"rsb_s{s}", rsb_sb[:])
                            dump(f"rbb_s{s}", rbb_sb[:])
                        em.wait("T", sV, v_msk)
                        if tt >= 1:
                            em.wait("T", sV, tile_trc[tt - 1])
                        em.op("T", lambda e: e.transpose(
                            ps_mm[6][0:NE, 0:128], mask_f[:], identf_sb[:]))
                        em.op("T", lambda e: e.transpose(
                            ps_mm[7][0:1, 0:128], cols[:, 4:5], identf_sb[:]))
                        t_tr = mark("T", sT)
                        tile_ttr[tt] = t_tr
                        em.wait("V", sT, t_tr)
                        em.op("V", lambda e, tt=tt: e.tensor_copy(
                            maskT_sb[:, tt * 128:(tt + 1) * 128], ps_mm[6][0:NE, 0:128]))
                        em.op("V", lambda e, tt=tt: e.tensor_copy(
                            r_hw[:, tt * 128:(tt + 1) * 128], ps_mm[7][0:1, 0:128]))
                        v_trc = mark("V", sV)
                        tile_trc[tt] = v_trc
                    dump(f"maskT_s{s}", maskT_sb[:])
                    dump(f"rhw_s{s}", r_hw)
                    v_ln = apply_fn()
                    bc_guard = [0, 0, 0, 0]
                    mbc_src = [None] * NE
                    for me in range(NE + 1):
                        bank = ps_mm[4 + (me % 4)]
                        if bc_guard[me % 4]:
                            eng, val = bc_guard[me % 4]
                            em.wait("T", sV if eng == "V" else sS, val)
                        if me == 0:
                            em.op("T", lambda e, bank=bank: e.matmul(
                                bank[:], halfr_f[:], r_hw,
                                start=True, stop=True, skip_group_check=True))
                        else:
                            em.op("T", lambda e, me=me, bank=bank: e.matmul(
                                bank[:], esel_sb[:, (me - 1) * 128: me * 128], maskT_sb[:],
                                start=True, stop=True, skip_group_check=True))
                        t_mb = mark("T", sT)
                        ceng = "V" if me % 2 == 0 else "S"
                        em.wait(ceng, sT, t_mb)
                        if me == 0:
                            em.op("V", lambda e, bank=bank: e.tensor_copy(
                                hwbc_sb[:], bank[:]))
                        elif ceng == "V":
                            em.op("V", lambda e, me=me, bank=bank: e.tensor_copy(
                                mbc_sb[:, (me - 1) * SLAB: me * SLAB], bank[:]))
                        else:
                            em.op("S", lambda e, me=me, bank=bank: e.copy(
                                mbc_sb[:, (me - 1) * SLAB: me * SLAB], bank[:]))
                        mk = mark(ceng, sS if ceng == "S" else sV)
                        bc_guard[me % 4] = (ceng, mk)
                        if me >= 1:
                            mbc_src[me - 1] = (ceng, mk)
                    v_mbc = sV.n
                    s_mbc = sS.n
                    em.wait("T", sS, s_mbc)
                    dump(f"mbc_s{s}", mbc_sb[:])
                    dump(f"hwbc_s{s}", hwbc_sb[:])
                    em.wait("T", sV, v_mbc)
                    em.wait("T", sG, ld_small_g)
                    for m in range(8):
                        em.op("T", lambda e, m=m: e.matmul(
                            ps_mm[m][:], eb_bf[:, m * 128:(m + 1) * 128], maskT_sb[:],
                            start=True, stop=False, skip_group_check=True))
                    t_chunk = [0] * (NE * C)
                    wv_v = [0] * (NE * C)
                    casting = cast and s == 0

                    def w_load(j):
                        par4 = j % 6
                        if casting and j % 2 == 0:
                            pc2 = (j // 2) % 2
                            em.wait("G", sY, cast_done[j - 3] if j >= 4 else 0)
                            cast_g[j] = g_dma(
                                Wcast[pc2][:].rearrange("p (c n) -> p c n", c=2),
                                ew_d.rearrange("e (c p) n -> (e c) p n", p=128)
                                [j:j + 2].rearrange("i p n -> p i n"))
                            em.wait("Y", sG, cast_g[j])
                            cast_done[j] = cast_done[j + 1] = y_dma(
                                w_scr.rearrange("(i p) n -> i p n", p=128)
                                [j:j + 2].rearrange("i p n -> p i n"),
                                Wcast[pc2][:].rearrange("p (c n) -> p c n", c=2))
                        em.wait("Y", sY, cast_done[j])
                        em.wait("Y", sT, t_chunk[j - 6] if j >= 6 else wbuf_guard[par4])
                        wv_v[j] = y_dma(Wbuf[par4][:], scr_r[j])

                    for j0 in range(5):
                        w_load(j0)
                    for i in range(NE * C):
                        e_i, k_i = divmod(i, C)
                        par = i % 2
                        if i >= 2:
                            em.wait("V", sT, t_chunk[i - 2])
                        if mbc_src[e_i] and mbc_src[e_i][0] == "S":
                            em.wait("V", sS, mbc_src[e_i][1])
                        em.op("V", lambda e, e_i=e_i, k_i=k_i, par=par: e.tensor_mul(
                            xe_buf[par][:],
                            xln_buf[:, k_i * SLAB:(k_i + 1) * SLAB],
                            mbc_sb[:, e_i * SLAB:(e_i + 1) * SLAB]))
                        xe_v = mark("V", sV)
                        if i + 5 < NE * C:
                            w_load(i + 5)
                        em.wait("T", sV, xe_v)
                        em.wait("T", sY, wv_v[i])
                        for m in range(8):
                            em.op("T", lambda e, m=m, par=par, par4=(i % 6),
                                  last=(i == NE * C - 1): e.matmul(
                                ps_mm[m][:], Wbuf[par4][:, m * 128:(m + 1) * 128],
                                xe_buf[par][:],
                                start=False, stop=last,
                                skip_group_check=True))
                        t_chunk[i] = mark("T", sT)
                    wbuf_guard = [t_chunk[max(i2 for i2 in range(NE * C) if i2 % 6 == b)]
                                  for b in range(6)]
                    v_sq_next = None
                    if s + 1 < NSLAB:
                        v_sq_next = emit_ln_sq((s + 1) * SLAB, xbf_buf, sq_buf)
                    em.wait("V", sT, t_chunk[-1])
                    ev_done = {}
                    for m in (0, 1, 4, 5, 2, 3, 6, 7):
                        em.op("V", lambda e, m=m: e.tensor_mul(tmp_buf[:], ps_mm[m][:], hwbc_sb[:]))
                        em.op("V", lambda e, m=m, t0=t0: e.tensor_add(
                            x_chunk(m, t0, SLAB), x_chunk(m, t0, SLAB), tmp_buf[:]))
                        ev_done[m] = mark("V", sV)
                    v_evac = ev_done[7]
                    ffn_bank_free[0] = ev_done
                    em.wait("T", sV, ev_done[1])
                    dump(f"xpost_s{s}", x_res[:])
                    if store:
                        em.wait("G", sV, v_evac)
                        g_dma(yT.rearrange("(c p) t -> p c t", p=128)[:, :, t0:t0 + SLAB],
                              x_res[:].rearrange("p (c t) -> p c t", c=C)[:, :, t0:t0 + SLAB])

        em.wait("V", sY, ld_x)


        if "ffn1" in stages:
            emit_ffn(store=False, cast=True)
            barrier()

        if "attn" in stages:
            ffn_bank_free[0] = None
            with contextlib.ExitStack() as pes:
                pc = pes.enter_context
                psA2 = pc(nc.psum_tensor("l2_a", [128, SLAB], F32))
                psB2 = pc(nc.psum_tensor("l2_b", [128, SLAB], F32))
                ps_s0 = pc(nc.psum_tensor("l2_s0", [1, SLAB], F32))
                ps_s1 = pc(nc.psum_tensor("l2_s1", [1, SLAB], F32))
                ps_p = [pc(nc.psum_tensor(f"pr{i}", [128, 512], F32)) for i in range(4)]
                xbf2 = pc(nc.sbuf_tensor("l2_xbf", [128, C * SLAB], BF16))
                sq2 = xbf2
                tmp2 = pc(nc.sbuf_tensor("l2_tmp", [128, SLAB], F32))
                wq_bf = pc(nc.sbuf_tensor("wq_bf", [128, C * DC], BF16))
                wkv_bf = pc(nc.sbuf_tensor("wkv_bf", [128, C * DC], BF16))
                wqr_bf = pc(nc.sbuf_tensor("wqr_bf", [128, C * DR], BF16))
                wkr_bf = pc(nc.sbuf_tensor("wkr_bf", [128, C * DR], BF16))
                cs_buf = pc(nc.sbuf_tensor("cs_buf", [128, 1024], F32))
                rr_buf = pc(nc.sbuf_tensor("rr_buf", [128, 1024], F32))
                rt_buf = pc(nc.sbuf_tensor("rt_buf", [128, 512], F32))

                gq = g_dma(wq_bf[:].rearrange("p (c n) -> p c n", c=C), r3(wq_d))
                gkv = g_dma(wkv_bf[:].rearrange("p (c n) -> p c n", c=C), r3(wkv_d))
                gqr = g_dma(wqr_bf[:].rearrange("p (c n) -> p c n", c=C), r3(wqr_d))
                gkr = g_dma(wkr_bf[:].rearrange("p (c n) -> p c n", c=C), r3(wkr_d))

                evac_hist = []
                rope_hist = []
                rope_v_hist = []
                for s in range(NSLAB):
                    t0 = s * SLAB
                    n = s
                    v_ln = emit_ln(t0, lambda c, t0=t0: A[:, c * S + t0: c * S + t0 + SLAB],
                                   psA2[:], psB2[:], ps_s0[0:1, :], ps_s1[0:1, :],
                                   xbf2, sq2, tmp2, inplace_sq=True)
                    em.wait("T", sG, gkv)
                    em.wait("T", sV, v_ln)
                    for (wbf, bcol, dst) in ((wq_bf, bq_c, cq_sb), (wkv_bf, bkv_c, ckv_sb)):
                        for m2 in range(2):
                            ps = ps_p[len(evac_hist) % 4]
                            if len(evac_hist) >= 4:
                                em.wait("T", sS, evac_hist[-4])
                            for k in range(C):
                                em.op("T", lambda e, ps=ps, wbf=wbf, m2=m2, n=n, k=k: e.matmul(
                                    ps[:], wbf[:, k * DC + m2 * 128: k * DC + m2 * 128 + 128],
                                    A[:, k * S + n * 512: k * S + n * 512 + 512],
                                    start=(k == 0), stop=(k == C - 1), skip_group_check=True))
                            t_p = mark("T", sT)
                            em.wait("S", sT, t_p)
                            em.op("S", lambda e, ps=ps, dst=dst, bcol=bcol, m2=m2, n=n:
                                  e.activation(
                                      dst[:, m2 * S + n * 512: m2 * S + n * 512 + 512],
                                      ps[:], AF.Gelu, bias=bcol[:, m2:m2 + 1]))
                            evac_hist.append(mark("S", sS))
                    for (wbf, bcol, dstg) in ((wqr_bf, bqr_c, qrg_sb), (wkr_bf, bkr_c, krg_sb)):
                        ps = ps_p[len(evac_hist) % 4]
                        if len(evac_hist) >= 4:
                            em.wait("T", sS, evac_hist[-4])
                        for k in range(C):
                            em.op("T", lambda e, ps=ps, wbf=wbf, n=n, k=k: e.matmul(
                                ps[:], wbf[:, k * DR:(k + 1) * DR],
                                A[:, k * S + n * 512: k * S + n * 512 + 512],
                                start=(k == 0), stop=(k == C - 1), skip_group_check=True))
                        t_p = mark("T", sT)
                        cso = 0
                        if len(rope_v_hist) >= 1:
                            em.wait("Y", sV, rope_v_hist[-1])
                        y_dma(cs_buf[:, cso:cso + 512], cosT_d[:, n * 512:(n + 1) * 512])
                        ys = y_dma(cs_buf[:, cso + 512:cso + 1024],
                                   srot_d[:, n * 512:(n + 1) * 512])
                        em.wait("S", sT, t_p)
                        em.op("S", lambda e, ps=ps, bcol=bcol: e.activation(
                            rr_buf[:, 0:512], ps[:], AF.Identity, bias=bcol[:, 0:1]))
                        s_p = mark("S", sS)
                        evac_hist.append(s_p)
                        em.wait("V", sS, s_p)
                        em.wait("V", sY, ys)
                        em.op("V", lambda e: e.tensor_copy(
                            rt_buf[0:64, :], rr_buf[64:128, 0:512]))
                        em.op("V", lambda e: e.tensor_copy(
                            rt_buf[64:128, :], rr_buf[0:64, 0:512]))
                        em.op("V", lambda e, cso=cso: e.tensor_mul(
                            rt_buf[:], rt_buf[:], cs_buf[:, cso + 512:cso + 1024]))
                        em.op("V", lambda e, cso=cso: e.tensor_mul(
                            rr_buf[:, 512:1024], rr_buf[:, 0:512], cs_buf[:, cso:cso + 512]))
                        em.op("V", lambda e: e.tensor_add(
                            rr_buf[:, 512:1024], rr_buf[:, 512:1024], rt_buf[:]))
                        v_rope = mark("V", sV)
                        rope_v_hist.append(v_rope)
                        em.wait("S", sV, v_rope)
                        em.op("S", lambda e, dstg=dstg, n=n: e.activation(
                            dstg[:, n * 512:(n + 1) * 512], rr_buf[:, 512:1024], AF.Gelu))
                        s_g = mark("S", sS)
                        rope_hist.append(s_g)
                        em.wait("V", sS, s_g)
                barrier()

            with contextlib.ExitStack() as hes:
                hc = hes.enter_context
                ps_sc = [hc(nc.psum_tensor(f"h_sc{i}", [128, 1024], F32)) for i in range(2)]
                ps_out = [hc(nc.psum_tensor(f"h_out{i}", [128, 1024], F32)) for i in range(2)]
                Qbuf = hc(nc.sbuf_tensor("Qbuf", [128, S], BF16))
                Kbuf = hc(nc.sbuf_tensor("Kbuf", [128, S], BF16))
                Vbuf = hc(nc.sbuf_tensor("Vbuf", [128, NT * 256], BF16))
                strips_all = hc(nc.sbuf_tensor("strips_all", [128, 6144], BF16))
                strip = [strips_all[:, i * 1024:(i + 1) * 1024] for i in range(6)]
                sqh = strips_all[:, 0:S]
                rbc_sb = hc(nc.sbuf_tensor("rbc_sb", [128, 1024], BF16))
                rrow_f = rows[:, 0:1024]
                rrow = rows[:].bitcast(BF16)[:, 4096:5120]
                mcol = hc(nc.sbuf_tensor("mcol", [2, 12], F32))

                em.op("V", lambda e: e.memset(Vbuf[:], 0.0))
                em.op("V", lambda e: e.memset(
                    Vbuf[:].rearrange("p (t c) -> p t c", t=NT)[:, :, 64:65], 1.0))
                em.op("V", lambda e: e.memset(
                    Vbuf[:].rearrange("p (t c) -> p t c", t=NT)[:, :, 128:129], 1.0))
                v_ones = mark("V", sV)
                em.wait("T", sV, v_ones)

                pso_guard = [0, 0]

                def build_pair(P):
                    vals = {}
                    em.wait("S", sT, sT.n)
                    em.wait("V", sT, sT.n)
                    em.wait("V", sS, sS.n)
                    em.wait("T", sS, sS.n)
                    em.wait("T", sV, sV.n)
                    if P == 7:
                        qb, kb = qrg_sb, krg_sb
                    else:
                        qb, kb = Qbuf, Kbuf
                        bhist = []
                        for (wbf, src, bcol, dst) in ((wqc_bf, cq_sb, bqc_c, Qbuf),
                                                      (wkc_bf, ckv_sb, bkc_c, Kbuf)):
                            for n in range(4):
                                ps = ps_sc[n % 2][:, 0:512]
                                if len(bhist) >= 2:
                                    em.wait("T", sS, bhist[-2])
                                for p in range(2):
                                    h = 2 * P + p
                                    for kc in range(2):
                                        em.op("T", lambda e, ps=ps, wbf=wbf, h=h, p=p, n=n,
                                              kc=kc, src=src: e.matmul(
                                            ps[64 * p:64 * p + 64, :],
                                            wbf[:, kc * EKR + h * 64: kc * EKR + h * 64 + 64],
                                            src[:, kc * S + n * 512: kc * S + n * 512 + 512],
                                            start=(kc == 0), stop=(kc == 1),
                                            skip_group_check=True))
                                t_q = mark("T", sT)
                                em.wait("S", sT, t_q)
                                em.op("S", lambda e, ps=ps, dst=dst, P=P, n=n,
                                      bcol=bcol: e.activation(
                                    dst[:, n * 512:(n + 1) * 512], ps,
                                    AF.Gelu, bias=bcol[:, P:P + 1]))
                                bhist.append(mark("S", sS))
                        em.wait("T", sS, sS.n)
                    vals["q"] = sS.n
                    em.wait("T", sG, ld_small_g)
                    vhist = []
                    for tt in range(NT):
                        ps = ps_sc[tt % 2][:, 0:128]
                        if len(vhist) >= 2:
                            em.wait("T", sV, vhist[-2])
                        for kc in range(2):
                            em.op("T", lambda e, tt=tt, kc=kc, P=P, ps=ps: e.matmul(
                                ps, ckv_sb[:, kc * S + tt * 128: kc * S + tt * 128 + 128],
                                wv_bf[:, kc * E + P * 128: kc * E + P * 128 + 128],
                                start=(kc == 0), stop=False, skip_group_check=True))
                        em.op("T", lambda e, P=P, ps=ps: e.matmul(
                            ps, onesr_bf[:], bv_row[:, P * 128:(P + 1) * 128],
                            start=False, stop=True, skip_group_check=True))
                        t_v = mark("T", sT)
                        em.wait("V", sT, t_v)
                        em.op("V", lambda e, tt=tt, ps=ps: e.tensor_copy(
                            Vbuf[:, tt * 256: tt * 256 + 64], ps[:, 0:64]))
                        em.op("V", lambda e, tt=tt, ps=ps: e.tensor_copy(
                            Vbuf[:, tt * 256 + 192: tt * 256 + 256], ps[:, 64:128]))
                        vhist.append(mark("V", sV))
                    vals["v"] = sV.n
                    em.wait("T", sV, sV.n)
                    for (buf, col) in ((qb, 0), (kb, 1)):
                        em.op("V", lambda e, buf=buf: e.tensor_mul(sqh[:], buf[:], buf[:]))
                        v_s = mark("V", sV)
                        em.wait("T", sV, v_s)
                        for n in range(4):
                            em.op("T", lambda e, n=n: e.matmul(
                                ps_sc[0][0:2, 0:512], bo2_sb[:],
                                sqh[:, n * 512:(n + 1) * 512],
                                start=True, stop=True, skip_group_check=True))
                            t_n = mark("T", sT)
                            em.wait("V", sT, t_n)
                            em.op("V", lambda e, n=n: e.reduce_max(
                                mcol[0:2, 4 + n:5 + n], ps_sc[0][0:2, 0:512], axis=AX.X))
                            v_n = mark("V", sV)
                            em.wait("T", sV, v_n)
                        v_parts = sV.n
                        em.wait("S", sV, v_parts)
                        em.op("S", lambda e: e.copy(mcol[0:2, 8:12], mcol[0:2, 4:8]))
                        s_parts = mark("S", sS)
                        em.wait("V", sS, s_parts)
                        em.op("V", lambda e: e.tensor_max(
                            mcol[0:2, 8:9], mcol[0:2, 8:9], mcol[0:2, 9:10]))
                        em.op("V", lambda e: e.tensor_max(
                            mcol[0:2, 10:11], mcol[0:2, 10:11], mcol[0:2, 11:12]))
                        em.op("V", lambda e, col=col: e.tensor_max(
                            mcol[0:2, col:col + 1], mcol[0:2, 8:9], mcol[0:2, 10:11]))
                    em.op("V", lambda e: e.tensor_add(
                        mcol[0:2, 2:3], mcol[0:2, 0:1], mcol[0:2, 1:2]))
                    em.op("V", lambda e: e.tensor_scalar(
                        mcol[0:2, 3:4], mcol[0:2, 2:3], -0.0625, -1.0,
                        op0=ALU.mult, op1=ALU.add))
                    v_nb = mark("V", sV)
                    em.wait("T", sV, v_nb)
                    em.op("T", lambda e: e.transpose(
                        ps_sc[1][0:1, 0:2], mcol[0:2, 3:4], identf_sb[0:2, 0:2]))
                    t_tr = mark("T", sT)
                    em.wait("V", sT, t_tr)
                    em.op("V", lambda e: e.tensor_copy(nb_row[:], ps_sc[1][0:1, 0:2]))
                    v_nr = mark("V", sV)
                    em.wait("T", sV, v_nr)
                    for p in range(2):
                        em.op("T", lambda e, p=p: e.matmul(
                            ps_sc[1][:, 4 + p:5 + p], onesr_f[:], nb_row[:, p:p + 1],
                            start=True, stop=True, skip_group_check=True))
                    t_bc = mark("T", sT)
                    em.wait("V", sT, t_bc)
                    for p in range(2):
                        em.op("V", lambda e, p=p: e.tensor_copy(
                            negB[p][:], ps_sc[1][:, 4 + p:5 + p]))
                    vals["negB"] = mark("V", sV)
                    em.wait("T", sV, vals["negB"])
                    em.wait("S", sV, vals["negB"])
                    return qb, kb, vals

                for P in range(8):
                    qb, kb, vals = build_pair(P)
                    last_half_se = [{0: 0, 1: 0}, {0: 0, 1: 0}]
                    last_pv = [0, 0]
                    for Hh in range(2):
                        q0, q1 = Hh * 1024, (Hh + 1) * 1024
                        kt_max = q1 // 128
                        t_s = [[{} for _ in range(kt_max)] for _ in range(2)]
                        s_e = [[{} for _ in range(kt_max)] for _ in range(2)]
                        t_pv = [[0] * kt_max, [0] * kt_max]
                        v_m = [[0] * kt_max, [0] * kt_max]
                        started = [[False, False], [False, False]]

                        def espan(kt):
                            ks = kt * 128
                            qs = max(q0, ks)
                            return ks, qs, q1 - qs

                        def halves(kt):
                            ks, qs, span = espan(kt)
                            out = []
                            off = 0
                            while off < span:
                                out.append((off, min(512, span - off)))
                                off += 512
                            return out

                        def eS(p, kt, off, ln):
                            ks, qs, span = espan(kt)
                            h2 = off // 512
                            em.wait("T", sS, last_half_se[p][h2])
                            em.op("T", lambda e, kt=kt, ks=ks, qs=qs, off=off,
                                  ln=ln, p=p: e.matmul(
                                ps_sc[p][:, off:off + ln],
                                kb[64 * p:64 * p + 64, ks:ks + 128],
                                qb[64 * p:64 * p + 64, qs + off: qs + off + ln],
                                start=True, stop=True, skip_group_check=True))
                            t_s[p][kt][h2] = mark("T", sT)

                        def eE(p, kt, off, ln):
                            ks, qs, span = espan(kt)
                            h2 = off // 512
                            em.wait("S", sT, t_s[p][kt][h2])
                            if kt >= 3:
                                em.wait("S", sT, t_pv[p][kt - 3])
                            else:
                                em.wait("S", sT, last_pv[p])
                            em.op("S", lambda e, kt=kt, off=off, ln=ln, p=p: e.activation(
                                strip[3 * p + kt % 3][:, off:off + ln],
                                ps_sc[p][:, off:off + ln],
                                AF.Exp, bias=negB[p][:], scale=0.125))
                            s_e[p][kt][h2] = mark("S", sS)
                            last_half_se[p][h2] = s_e[p][kt][h2]
                            if qs == ks and off == 0:
                                em.wait("V", sS, s_e[p][kt][h2])
                                em.op("V", lambda e, kt=kt, p=p: e.tensor_mul(
                                    strip[3 * p + kt % 3][:, 0:128],
                                    strip[3 * p + kt % 3][:, 0:128], tri_sb[:]))
                                v_m[p][kt] = mark("V", sV)

                        def ePV(p, kt):
                            ks, qs, span = espan(kt)
                            for h2, (off, ln) in enumerate(halves(kt)):
                                em.wait("T", sS, s_e[p][kt][h2])
                            if v_m[p][kt]:
                                em.wait("T", sV, v_m[p][kt])
                            for nb in range(2):
                                lo = max(qs, q0 + nb * 512)
                                hi = q0 + (nb + 1) * 512
                                if lo >= hi:
                                    continue
                                st = not started[p][nb]
                                if st:
                                    em.wait("T", sV, pso_guard[p])
                                em.op("T", lambda e, kt=kt, qs=qs, lo=lo, hi=hi,
                                      p=p, st=st, q0=q0: e.matmul(
                                    ps_out[p][:, lo - q0: hi - q0],
                                    Vbuf[:, kt * 256 + p * 128: kt * 256 + p * 128 + 128],
                                    strip[3 * p + kt % 3][:, lo - qs: hi - qs],
                                    start=st, stop=False, skip_group_check=True))
                                started[p][nb] = True
                            t_pv[p][kt] = mark("T", sT)

                        for kt in range(kt_max):
                            hs = halves(kt)
                            for (off, ln) in hs:
                                eS(0, kt, off, ln)
                                eS(1, kt, off, ln)
                                eE(0, kt, off, ln)
                                eE(1, kt, off, ln)
                            if kt >= 2:
                                ePV(0, kt - 2)
                                ePV(1, kt - 2)
                        for kt in (kt_max - 2, kt_max - 1):
                            ePV(0, kt)
                            ePV(1, kt)
                        last_pv = [t_pv[0][kt_max - 1], t_pv[1][kt_max - 1]]
                        for p in range(2):
                            h = 2 * P + p
                            pso = ps_out[p]
                            drow = 64 if p == 0 else 0
                            brow = 64 if p == 0 else 0
                            data0 = 0 if p == 0 else 64
                            em.wait("S", sT, t_pv[p][kt_max - 1])
                            em.op("S", lambda e, pso=pso, drow=drow: e.activation(
                                rrow_f[:], pso[drow:drow + 1, :], AF.Ln))
                            em.op("S", lambda e: e.activation(
                                rrow[:], rrow_f[:], AF.Exp, scale=-1.0))
                            s_rr = mark("S", sS)
                            em.wait("T", sS, s_rr)
                            for nb2 in range(2):
                                em.op("T", lambda e, nb2=nb2, pso=pso, brow=brow: e.matmul(
                                    pso[brow:brow + 64, nb2 * 512:(nb2 + 1) * 512],
                                    onesr_bf[0:1, 0:64],
                                    rrow[:, nb2 * 512:(nb2 + 1) * 512],
                                    start=True, stop=True, skip_group_check=True))
                            t_rb = mark("T", sT)
                            em.wait("V", sT, t_rb)
                            em.op("V", lambda e, pso=pso, brow=brow, data0=data0:
                                  e.tensor_copy(rbc_sb[data0:data0 + 64, :],
                                                pso[brow:brow + 64, :]))
                            for nb2 in range(2):
                                em.op("V", lambda e, pso=pso, nb2=nb2, h=h, Hh=Hh,
                                      data0=data0: e.tensor_mul(
                                    A[data0:data0 + 64,
                                      (h // 2) * S + Hh * 1024 + nb2 * 512:
                                      (h // 2) * S + Hh * 1024 + nb2 * 512 + 512],
                                    pso[data0:data0 + 64, nb2 * 512:(nb2 + 1) * 512],
                                    rbc_sb[data0:data0 + 64,
                                           nb2 * 512:(nb2 + 1) * 512]))
                            pso_guard[p] = mark("V", sV)
                dump("Aattn", A[:])
                barrier()


            with contextlib.ExitStack() as oes:
                oc = oes.enter_context
                ps_o = [oc(nc.psum_tensor(f"wo{m}", [128, 512], F32)) for m in range(8)]
                wo_buf = [oc(nc.sbuf_tensor(f"wo_b{i}", [128, E], BF16)) for i in range(2)]
                wo_r = wo_d.rearrange("(k p) n -> k p n", p=128)
                wo_read = [0, 0]
                for n in range(4):
                    for k in range(C):
                        par = k % 2
                        em.wait("G", sT, wo_read[par])
                        gv = g_dma(wo_buf[par][:], wo_r[k])
                        em.wait("T", sG, gv)
                        for m in range(8):
                            em.op("T", lambda e, m=m, k=k, n=n, par=par: e.matmul(
                                ps_o[m][:], wo_buf[par][:, m * 128:(m + 1) * 128],
                                A[:, k * S + n * 512: k * S + n * 512 + 512],
                                start=(k == 0), stop=(k == C - 1), skip_group_check=True))
                        wo_read[par] = mark("T", sT)
                    em.wait("V", sT, wo_read[(C - 1) % 2])
                    for m in range(8):
                        em.op("V", lambda e, m=m, n=n: e.scalar_tensor_tensor(
                            x_chunk(m, n * 512, 512), ps_o[m][:], bo_c[:, m:m + 1],
                            x_chunk(m, n * 512, 512), op0=ALU.add, op1=ALU.add))
                    v_n = mark("V", sV)
                    em.wait("T", sV, v_n)
                barrier()

        if "ffn2" in stages:
            emit_ffn(store=True, tag="b")
        else:
            barrier()
            for s in range(NSLAB):
                t0 = s * SLAB
                g_dma(yT.rearrange("(c p) t -> p c t", p=128)[:, :, t0:t0 + SLAB],
                      x_res[:].rearrange("p (c t) -> p c t", c=C)[:, :, t0:t0 + SLAB])

        em.wait("G", sG, sG.n)
        em.wait("G", sY, sY.n)

        block = ec(nc.Block())
        block.tensor(lambda eng: [f(eng) for f in em.q["T"]])
        block.vector(lambda eng: [f(eng) for f in em.q["V"]])
        block.scalar(lambda eng: [f(eng) for f in em.q["S"]])
        block.gpsimd(lambda eng: [f(eng) for f in em.q["G"]])
        block.sync(lambda eng: [f(eng) for f in em.q["Y"]])

    return nc


_CACHE = {}


def _program(stages=_STAGES):
    key = tuple(stages)
    if key not in _CACHE:
        nc = bass.Bass("TRN2", target_bir_lowering=False)
        build(nc, stages=stages)
        _CACHE[key] = nc
    return _CACHE[key]


def _in_maps(inputs):
    g = lambda k: np.asarray(inputs[k], dtype=np.float32)
    shared = dict(
        wq=g("L0_attn_wq"), bq=g("L0_attn_bq").reshape(DC, 1),
        wqc=g("L0_attn_wqc"), bqc=g("L0_attn_bqc").reshape(EKR, 1),
        wqr=g("L0_attn_wqr"), bqr=g("L0_attn_bqr").reshape(DR, 1),
        wkv=g("L0_attn_wkv"), bkv=g("L0_attn_bkv").reshape(DC, 1),
        wkr=g("L0_attn_wkr"), bkr=g("L0_attn_bkr").reshape(DR, 1),
        wkc=g("L0_attn_wkc"), bkc=g("L0_attn_bkc").reshape(EKR, 1),
        wv=g("L0_attn_wv"), bv=g("L0_attn_bv").reshape(1, E),
        wo=g("L0_attn_wo"), bo=g("L0_attn_bo").reshape(E, 1),
        rwT=np.ascontiguousarray(g("L0_ffn_router_w").T),
        rb=g("L0_ffn_router_b").reshape(1, NE),
        ewT=np.ascontiguousarray(g("L0_ffn_expert_w").transpose(0, 2, 1)),
        eb=g("L0_ffn_expert_b"),
    )
    x = g("L0_attn_x")
    maps = []
    for i in range(8):
        m = dict(shared)
        m["xT"] = np.ascontiguousarray(x[i].T)
        maps.append(m)
    return maps


def _run(inputs, stages=_STAGES):
    nc = _program(stages)
    res = run_bass_kernel_spmd(nc, _in_maps(inputs), core_ids=list(range(8)))
    return np.stack([np.asarray(r["yT"], dtype=np.float32).T for r in res.results], axis=0)


def kernel(**inputs):
    return _run(inputs)


# version 10
# speedup vs baseline: 1.0220x; 1.0220x over previous
import contextlib
import numpy as np
import concourse.bass as bass
import concourse.mybir as mybir
from concourse.bass_utils import run_bass_kernel_spmd

F32 = mybir.dt.float32
BF16 = mybir.dt.bfloat16
AF = mybir.ActivationFunctionType
ALU = mybir.AluOpType
AX = mybir.AxisListType

S = 2048
E = 1024
C = 8
NH = 16
HD = 64
DC = 256
DR = 128
EKR = E - DR
NE = 8
SLAB = 512
NSLAB = S // SLAB
NT = S // 128
EPS = 1e-5

_STAGES = ("ffn1", "attn", "ffn2")
_DBG = []
_DBG_SHAPES = {}


class Sem:
    def __init__(self, h):
        self.h = h
        self.n = 0

    def inc(self, k=1):
        self.n += k
        return self.n


class Em:

    ENGS = ("T", "V", "S", "G", "Y")

    def __init__(self):
        self.q = {e: [] for e in self.ENGS}
        self.waited = {e: {} for e in self.ENGS}

    def op(self, e, fn):
        self.q[e].append(fn)

    def wait(self, e, sem, val):
        if val <= 0:
            return
        w = self.waited[e]
        if w.get(id(sem), 0) >= val:
            return
        w[id(sem)] = val
        self.q[e].append(lambda eng, s=sem, v=val: eng.wait_ge(s.h, v))


def build(nc, stages=_STAGES):
    def din(name, shape):
        return nc.dram_tensor(name, list(shape), F32, kind="ExternalInput").ap()

    xT = din("xT", (E, S))
    wq_d = din("wq", (E, DC)); bq_d = din("bq", (DC, 1))
    wqc_d = din("wqc", (DC, EKR)); bqc_d = din("bqc", (EKR, 1))
    wqr_d = din("wqr", (E, DR)); bqr_d = din("bqr", (DR, 1))
    wkv_d = din("wkv", (E, DC)); bkv_d = din("bkv", (DC, 1))
    wkr_d = din("wkr", (E, DR)); bkr_d = din("bkr", (DR, 1))
    wkc_d = din("wkc", (DC, EKR)); bkc_d = din("bkc", (EKR, 1))
    wv_d = din("wv", (DC, E)); bv_d = din("bv", (1, E))
    wo_d = din("wo", (E, E)); bo_d = din("bo", (E, 1))
    rw_d = din("rwT", (E, NE)); rb_d = din("rb", (1, NE))
    ew_d = din("ewT", (NE, E, E))
    eb_d = din("eb", (NE, E))

    yT = nc.dram_tensor("yT", [E, S], F32, kind="ExternalOutput").ap()
    w_scr = nc.dram_tensor("w_scr", [NE * C * 128, E], BF16, kind="Internal").ap()

    import ml_dtypes
    bfdt = ml_dtypes.bfloat16
    ident_bf_d = nc.inline_tensor(np.eye(128, dtype=bfdt), "ident_bf_dr").ap()
    ident_f_d = nc.inline_tensor(np.eye(128, dtype=np.float32), "ident_f32_dr").ap()
    onesc_bf_d = nc.inline_tensor(np.ones((128, 1), dtype=bfdt), "onesc_bf_dr").ap()
    onesc_f_d = nc.inline_tensor(np.ones((128, 1), dtype=np.float32), "onesc_f_dr").ap()
    onesr_f_d = nc.inline_tensor(np.ones((1, 128), dtype=np.float32), "onesr_f_dr").ap()
    halfr_f_d = nc.inline_tensor(np.full((1, 128), 0.5, dtype=np.float32), "halfr_f_dr").ap()
    onesr_bf_d = nc.inline_tensor(np.ones((1, 128), dtype=bfdt), "onesr_bf_dr").ap()
    tri_d = nc.inline_tensor(np.triu(np.ones((128, 128), dtype=bfdt)), "tri_bf_dr").ap()
    bo2 = np.zeros((128, 2), dtype=bfdt)
    bo2[0:64, 0] = 1
    bo2[64:128, 1] = 1
    bo2_d = nc.inline_tensor(bo2, "bo2_bf_dr").ap()
    esel = np.zeros((NE, NE * 128), dtype=bfdt)
    for _e in range(NE):
        esel[_e, _e * 128:(_e + 1) * 128] = 1
    esel_d = nc.inline_tensor(esel, "esel_bf_dr").ap()
    eps_d = nc.inline_tensor(np.full((1, 1), EPS, dtype=np.float32), "eps_dr").ap()
    inv_freq = 1.0 / 10000.0 ** (np.arange(0, DR, 2, dtype=np.float64) / DR)
    t_pos = np.arange(S, dtype=np.float64)
    fr = np.concatenate([inv_freq, inv_freq])[:, None] * t_pos[None, :]
    cosT_np = np.cos(fr).astype(np.float32)
    srot_np = np.sin(fr).astype(np.float32)
    srot_np[0:64, :] = -srot_np[0:64, :]
    cosT_d = nc.inline_tensor(cosT_np, "cosT_dr").ap()
    srot_d = nc.inline_tensor(srot_np, "srotT_dr").ap()

    em = Em()

    with contextlib.ExitStack() as top:
        ec = top.enter_context
        sY = Sem(ec(nc.semaphore("sY")))
        sG = Sem(ec(nc.semaphore("sG")))
        sT = Sem(ec(nc.semaphore("sT")))
        sV = Sem(ec(nc.semaphore("sV")))
        sS = Sem(ec(nc.semaphore("sS")))
        sB = Sem(ec(nc.semaphore("sB")))

        def y_dma(dst, src):
            v = sY.inc(16)
            em.op("Y", lambda e, d=dst, s=src: e.dma_start(out=d, in_=s).then_inc(sY.h, 16))
            return v

        def g_dma(dst, src):
            v = sG.inc(16)
            em.op("G", lambda e, d=dst, s=src: e.dma_start(out=d, in_=s).then_inc(sG.h, 16))
            return v

        def mark(e, sem):
            v = sem.inc()
            prev = em.q[e][-1]

            def wrapped(eng, p=prev, s=sem):
                ins = p(eng)
                ins.then_inc(s.h, 1)
                return ins

            em.q[e][-1] = wrapped
            return v

        def barrier():
            for e in ("T", "V", "S"):
                em.op(e, lambda eng: eng.drain().then_inc(sB.h, 1))
                sB.inc()
            for e, dsem in (("G", sG), ("Y", sY)):
                em.wait(e, dsem, dsem.n)
                em.op(e, lambda eng: eng.sem_inc(sB.h, 1))
                sB.inc()
            tgt = sB.n
            for e in Em.ENGS:
                em.wait(e, sB, tgt)

        def dump(name, ap):
            if name not in _DBG:
                return
            shp = list(ap.shape)
            _DBG_SHAPES[name] = shp
            d = nc.dram_tensor("dbg_" + name, shp, ap.dtype, kind="ExternalOutput").ap()
            for sem in (sV, sS, sT):
                em.wait("G", sem, sem.n)
            gv = g_dma(d, ap)
            for e in ("T", "V", "S", "Y"):
                em.wait(e, sG, gv)

        x_res = ec(nc.sbuf_tensor("x_res", [128, C * S], F32))
        A = ec(nc.sbuf_tensor("A", [128, C * S], BF16))
        cq_sb = ec(nc.sbuf_tensor("cq_sb", [128, 2 * S], BF16))
        ckv_sb = ec(nc.sbuf_tensor("ckv_sb", [128, 2 * S], BF16))
        qrg_sb = ec(nc.sbuf_tensor("qrg_sb", [128, S], BF16))
        krg_sb = ec(nc.sbuf_tensor("krg_sb", [128, S], BF16))
        wqc_bf = ec(nc.sbuf_tensor("wqc_bf", [128, 2 * EKR], BF16))
        wkc_bf = ec(nc.sbuf_tensor("wkc_bf", [128, 2 * EKR], BF16))
        wv_bf = ec(nc.sbuf_tensor("wv_bf", [128, 2 * E], BF16))
        Wbig = ec(nc.sbuf_tensor("Wbig", [128, 6 * E], BF16))
        Wbuf = [Wbig[:, i * E:(i + 1) * E] for i in range(6)]
        Wpair = [Wbig[:, 2 * i * E:(2 * i + 2) * E] for i in range(3)]
        Wcast = [ec(nc.sbuf_tensor(f"Wcast{i}", [128, 2 * E], BF16)) for i in range(2)]
        bq_c = ec(nc.sbuf_tensor("bq_c", [128, 2], F32))
        bqc_c = ec(nc.sbuf_tensor("bqc_c", [128, 7], F32))
        bqr_c = ec(nc.sbuf_tensor("bqr_c", [128, 1], F32))
        bkv_c = ec(nc.sbuf_tensor("bkv_c", [128, 2], F32))
        bkr_c = ec(nc.sbuf_tensor("bkr_c", [128, 1], F32))
        bkc_c = ec(nc.sbuf_tensor("bkc_c", [128, 7], F32))
        bo_c = ec(nc.sbuf_tensor("bo_c", [128, 8], F32))
        bv_row = ec(nc.sbuf_tensor("bv_row", [1, E], BF16))
        rw_f = ec(nc.sbuf_tensor("rw_f", [128, C * NE], F32))
        rb_row = ec(nc.sbuf_tensor("rb_row", [1, NE], F32))
        eb_bf = ec(nc.sbuf_tensor("eb_bf", [NE, E], BF16))
        identbf_sb = ec(nc.sbuf_tensor("identbf_sb", [128, 128], BF16))
        identf_sb = ec(nc.sbuf_tensor("identf_sb", [128, 128], F32))
        onesc_bf = ec(nc.sbuf_tensor("onesc_bf", [128, 1], BF16))
        onesc_f = ec(nc.sbuf_tensor("onesc_f", [128, 1], F32))
        onesr_f = ec(nc.sbuf_tensor("onesr_f", [1, 128], F32))
        halfr_f = ec(nc.sbuf_tensor("halfr_f", [1, 128], F32))
        onesr_bf = ec(nc.sbuf_tensor("onesr_bf", [1, 128], BF16))
        tri_sb = ec(nc.sbuf_tensor("tri_sb", [128, 128], BF16))
        bo2_sb = ec(nc.sbuf_tensor("bo2_sb", [128, 2], BF16))
        esel_sb = ec(nc.sbuf_tensor("esel_sb", [NE, NE * 128], BF16))
        rows = ec(nc.sbuf_tensor("rows", [1, 6 * SLAB], F32))
        nb_row = ec(nc.sbuf_tensor("nb_row", [1, 2], F32))
        eps_sb = ec(nc.sbuf_tensor("eps_sb", [1, 1], F32))
        negB = [ec(nc.sbuf_tensor(f"negB{p}", [128, 1], F32)) for p in range(2)]

        r_mean = rows[:, 0 * SLAB:1 * SLAB]
        r_msq = rows[:, 1 * SLAB:2 * SLAB]
        r_var = rows[:, 2 * SLAB:3 * SLAB]
        r_rstd = rows[:, 3 * SLAB:4 * SLAB]
        r_b = rows[:, 4 * SLAB:5 * SLAB]
        r_hw = rows[:, 5 * SLAB:6 * SLAB]

        def r3(ap_):
            return ap_.rearrange("(c p) n -> p c n", p=128)

        ld_x = y_dma(x_res[:].rearrange("p (c t) -> p c t", c=C), r3(xT))
        for col, src in ((bq_c, bq_d), (bqc_c, bqc_d), (bkv_c, bkv_d),
                         (bkc_c, bkc_d), (bo_c, bo_d)):
            n = src.shape[0] // 128
            for cc in range(n):
                y_dma(col[:, cc:cc + 1], src[cc * 128:(cc + 1) * 128, 0:1])
        y_dma(bqr_c[:], bqr_d)
        y_dma(bkr_c[:], bkr_d)
        y_dma(rb_row[:], rb_d)
        y_dma(identbf_sb[:], ident_bf_d)
        y_dma(identf_sb[:], ident_f_d)
        y_dma(onesc_bf[:], onesc_bf_d)
        y_dma(onesc_f[:], onesc_f_d)
        y_dma(onesr_f[:], onesr_f_d)
        y_dma(halfr_f[:], halfr_f_d)
        y_dma(onesr_bf[:], onesr_bf_d)
        y_dma(tri_sb[:], tri_d)
        y_dma(bo2_sb[:], bo2_d)
        y_dma(esel_sb[:], esel_d)
        y_dma(eps_sb[:], eps_d)
        ld_consts = sY.n

        g_dma(wqc_bf[:].rearrange("p (c n) -> p c n", c=2), r3(wqc_d))
        g_dma(wkc_bf[:].rearrange("p (c n) -> p c n", c=2), r3(wkc_d))
        g_dma(wv_bf[:].rearrange("p (c n) -> p c n", c=2), r3(wv_d))
        g_dma(bv_row[:], bv_d)
        ld_rw = y_dma(rw_f[:].rearrange("p (c n) -> p c n", c=C), r3(rw_d))
        g_dma(eb_bf[:], eb_d)
        ld_small_g = sG.n

        ew_r = ew_d.rearrange("e (k p) n -> e k p n", p=128)
        scr_r = w_scr.rearrange("(i p) n -> i p n", p=128)
        cast_done = [0] * (NE * C)
        cast_g = [0] * (NE * C)
        if "ffn1" not in stages:
            wbuf_free = [0, 0, 0, 0]
            for i in range(NE * C):
                e_i, k_i = divmod(i, C)
                par = i % 4
                em.wait("G", sY, wbuf_free[par])
                gv = g_dma(Wcast[par][:], ew_r[e_i, k_i])
                em.wait("Y", sG, gv)
                cast_done[i] = y_dma(scr_r[i], Wcast[par][:])
                wbuf_free[par] = cast_done[i]

        def x_chunk(c, t0, tl):
            return x_res[:, c * S + t0: c * S + t0 + tl]

        ffn_bank_free = [None]
        ln_stat_guard = [0, 0]

        def emit_ln_sq(t0, xbf_buf, sq_buf):
            for c in range(C):
                em.op("V", lambda e, c=c: e.tensor_copy(
                    xbf_buf[:, c * SLAB:(c + 1) * SLAB], x_chunk(c, t0, SLAB)))
                em.op("V", lambda e, c=c: e.tensor_mul(
                    sq_buf[:, c * SLAB:(c + 1) * SLAB],
                    xbf_buf[:, c * SLAB:(c + 1) * SLAB],
                    xbf_buf[:, c * SLAB:(c + 1) * SLAB]))
            return mark("V", sV)

        def emit_ln(t0, dst_fn, psA, psB, ps_s0, ps_s1, xbf_buf, sq_buf, tmp_buf,
                    v_sq=None, defer_apply=False, inplace_sq=False):
            em.wait("T", sS, ln_stat_guard[0])
            em.wait("T", sV, ln_stat_guard[1])
            if inplace_sq:
                for c in range(C):
                    em.op("V", lambda e, c=c: e.tensor_copy(
                        xbf_buf[:, c * SLAB:(c + 1) * SLAB], x_chunk(c, t0, SLAB)))
                v_cast = mark("V", sV)
                em.wait("T", sV, v_cast)
                em.wait("T", sY, ld_consts)
                for c in range(C):
                    em.op("T", lambda e, c=c: e.matmul(
                        ps_s0, onesc_bf[:], xbf_buf[:, c * SLAB:(c + 1) * SLAB],
                        start=(c == 0), stop=(c == C - 1), skip_group_check=True))
                t_r0 = mark("T", sT)
                em.wait("V", sT, t_r0)
                for c in range(C):
                    em.op("V", lambda e, c=c: e.tensor_mul(
                        xbf_buf[:, c * SLAB:(c + 1) * SLAB],
                        xbf_buf[:, c * SLAB:(c + 1) * SLAB],
                        xbf_buf[:, c * SLAB:(c + 1) * SLAB]))
                v_sq2 = mark("V", sV)
                em.wait("T", sV, v_sq2)
                for c in range(C):
                    em.op("T", lambda e, c=c: e.matmul(
                        ps_s1, onesc_bf[:], xbf_buf[:, c * SLAB:(c + 1) * SLAB],
                        start=(c == 0), stop=(c == C - 1), skip_group_check=True))
                t_red = mark("T", sT)
            else:
                if v_sq is None:
                    v_sq = emit_ln_sq(t0, xbf_buf, sq_buf)
                em.wait("T", sV, v_sq)
                em.wait("T", sY, ld_consts)
                for c in range(C):
                    em.op("T", lambda e, c=c: e.matmul(
                        ps_s0, onesc_bf[:], xbf_buf[:, c * SLAB:(c + 1) * SLAB],
                        start=(c == 0), stop=(c == C - 1), skip_group_check=True))
                for c in range(C):
                    em.op("T", lambda e, c=c: e.matmul(
                        ps_s1, onesc_bf[:], sq_buf[:, c * SLAB:(c + 1) * SLAB],
                        start=(c == 0), stop=(c == C - 1), skip_group_check=True))
                t_red = mark("T", sT)
            if ffn_bank_free[0]:
                em.wait("T", sV, ffn_bank_free[0][3])
            em.wait("S", sT, t_red)
            em.op("S", lambda e: e.activation(r_mean, ps_s0, AF.Copy, scale=1.0 / E))
            s_mean = mark("S", sS)
            ln_stat_guard[0] = s_mean
            em.wait("V", sS, s_mean)
            em.op("V", lambda e: e.tensor_mul(r_msq, r_mean, r_mean))
            em.op("V", lambda e: e.scalar_tensor_tensor(
                r_var, ps_s1, 1.0 / E, r_msq, op0=ALU.mult, op1=ALU.subtract))
            v_var = mark("V", sV)
            ln_stat_guard[1] = v_var
            em.wait("S", sV, v_var)
            em.op("S", lambda e: e.activation(r_rstd, r_var, AF.Ln, bias=eps_sb[0:1, 0:1]))
            em.op("S", lambda e: e.activation(r_rstd, r_rstd, AF.Exp, scale=-0.5))
            s_rstd = mark("S", sS)
            em.wait("V", sS, s_rstd)
            em.op("V", lambda e: e.scalar_tensor_tensor(
                r_b, r_mean, -1.0, r_rstd, op0=ALU.mult, op1=ALU.mult))
            v_b = mark("V", sV)
            em.wait("T", sS, s_rstd)
            em.op("T", lambda e: e.matmul(psA, onesr_f[:], r_rstd, start=True, stop=True,
                                          skip_group_check=True))
            em.wait("T", sV, v_b)
            em.op("T", lambda e: e.matmul(psB, onesr_f[:], r_b, start=True, stop=True,
                                          skip_group_check=True))
            t_bc = mark("T", sT)

            def apply():
                em.wait("V", sT, t_bc)
                for c in range(C):
                    em.op("V", lambda e, c=c: e.tensor_mul(
                        tmp_buf[:], x_chunk(c, t0, SLAB), psA))
                    em.op("V", lambda e, c=c: e.tensor_add(dst_fn(c), tmp_buf[:], psB))
                return mark("V", sV)

            if defer_apply:
                return apply
            return apply()

        def emit_ffn(store=False, tag="", cast=False):
            with contextlib.ExitStack() as fes:
                fc = fes.enter_context
                ps_mm = [fc(nc.psum_tensor(f"ps{tag}_mm{m}", [128, SLAB], F32)) for m in range(8)]
                xbf_buf = A[:, 0:C * SLAB]
                sq_buf = A[:, C * SLAB:2 * C * SLAB]
                tmp_buf = fc(nc.sbuf_tensor(f"f_tmp{tag}", [128, SLAB], F32))
                xln_buf = fc(nc.sbuf_tensor(f"f_xln{tag}", [128, C * SLAB], BF16))
                xe_buf = [fc(nc.sbuf_tensor(f"f_xe{tag}{i}", [128, SLAB], BF16)) for i in range(2)]
                maskT_sb = fc(nc.sbuf_tensor(f"f_maskT{tag}", [NE, SLAB], BF16))
                mask_f = fc(nc.sbuf_tensor(f"f_maskf{tag}", [128, NE], F32))
                mbc_sb = fc(nc.sbuf_tensor(f"f_mbc{tag}", [128, NE * SLAB], BF16))
                hwbc_sb = fc(nc.sbuf_tensor(f"f_hwbc{tag}", [128, SLAB], F32))
                e_sb = fc(nc.sbuf_tensor(f"f_e{tag}", [128, NE], F32))
                cols = fc(nc.sbuf_tensor(f"f_cols{tag}", [128, 8], F32))
                rsb_sb = fc(nc.sbuf_tensor(f"f_rsb{tag}", [128, NE], F32))
                rbb_sb = fc(nc.sbuf_tensor(f"f_rbb{tag}", [128, NE], F32))
                lg_sb = fc(nc.sbuf_tensor(f"f_lg{tag}", [128, NE], F32))
                lgf_sb = fc(nc.sbuf_tensor(f"f_lgf{tag}", [128, NE], F32))

                em.wait("T", sY, ld_consts)
                em.wait("T", sY, ld_rw)
                for c in range(C):
                    em.op("T", lambda e, c=c: e.matmul(
                        ps_mm[0][0:1, 0:NE], onesc_f[:], rw_f[:, c * NE:(c + 1) * NE],
                        start=(c == 0), stop=(c == C - 1), skip_group_check=True))
                t_rs = mark("T", sT)
                em.wait("V", sT, t_rs)
                em.op("V", lambda e: e.tensor_copy(rows[:, 0:NE], ps_mm[0][0:1, 0:NE]))
                v_rs = mark("V", sV)
                em.wait("T", sV, v_rs)
                em.op("T", lambda e: e.matmul(ps_mm[0][:, 0:NE], onesr_f[:], rows[:, 0:NE],
                                              start=True, stop=True, skip_group_check=True))
                em.op("T", lambda e: e.matmul(ps_mm[1][:, 0:NE], onesr_f[:], rb_row[:],
                                              start=True, stop=True, skip_group_check=True))
                t_rsb = mark("T", sT)
                em.wait("V", sT, t_rsb)
                em.op("V", lambda e: e.tensor_copy(rsb_sb[:], ps_mm[0][:, 0:NE]))
                em.op("V", lambda e: e.tensor_copy(rbb_sb[:], ps_mm[1][:, 0:NE]))
                v_rsb = mark("V", sV)
                em.wait("T", sV, v_rsb)
                wbuf_guard = [0] * 6
                v_sq_next = None
                ffn_bank_free[0] = None
                for s in range(NSLAB):
                    t0 = s * SLAB
                    apply_fn = emit_ln(
                        t0, lambda c: xln_buf[:, c * SLAB:(c + 1) * SLAB],
                        ps_mm[2][:], ps_mm[3][:],
                        ps_mm[0][0:1, :], ps_mm[1][0:1, :],
                        xbf_buf, sq_buf, tmp_buf, v_sq=v_sq_next, defer_apply=True)
                    dump(f"xln_s{s}", xln_buf[:])
                    dump(f"xbf_s{s}", xbf_buf[:])
                    em.op("V", lambda e: e.tensor_scalar_mul(r_msq, r_rstd, -1.0))
                    v_nr = mark("V", sV)
                    tile_cons = [0, 0, 0, 0]
                    tile_trc = [0, 0, 0, 0]
                    tile_ttr = [0, 0, 0, 0]
                    if ffn_bank_free[0]:
                        em.wait("T", sV, ffn_bank_free[0][5])
                    for tt in range(4):
                        psl = ps_mm[4 + (tt % 2)][:, 0:NE]
                        pst = ps_mm[6][:, 0:4]
                        if tt >= 2:
                            em.wait("T", sV, tile_cons[tt - 2])
                        for c in range(C):
                            em.op("T", lambda e, c=c, tt=tt, psl=psl, t0=t0: e.matmul(
                                psl, x_chunk(c, t0 + tt * 128, 128),
                                rw_f[:, c * NE:(c + 1) * NE],
                                start=(c == 0), stop=(c == C - 1), skip_group_check=True))
                        if tt == 0 and f"psl_pre_s{s}" in _DBG:
                            mark("T", sT)
                            em.wait("V", sT, sT.n)
                            em.op("V", lambda e, psl=psl: e.tensor_copy(lgf_sb[:], psl))
                            mark("V", sV)
                            dump(f"psl_pre_s{s}", lgf_sb[:])
                        em.wait("T", sV, v_nr)
                        em.op("T", lambda e, tt=tt, pst=pst: e.transpose(
                            pst[:, 0:1], r_mean[0:1, tt * 128:(tt + 1) * 128],
                            identf_sb[0:1, 0:1]))
                        em.op("T", lambda e, tt=tt, pst=pst: e.transpose(
                            pst[:, 1:2], r_msq[0:1, tt * 128:(tt + 1) * 128],
                            identf_sb[0:1, 0:1]))
                        t_lg = mark("T", sT)
                        if tt == 0 and f"psl_s{s}" in _DBG:
                            em.wait("V", sT, t_lg)
                            em.op("V", lambda e, psl=psl: e.tensor_copy(lg_sb[:], psl))
                            mark("V", sV)
                            dump(f"psl_s{s}", lg_sb[:])
                            dump(f"xck_s{s}", x_res[:, 0:128])
                            dump(f"rwf_s{s}", rw_f[:])
                        em.wait("S", sT, t_lg)
                        em.op("S", lambda e, pst=pst: e.copy(cols[:, 6:8], pst[:, 0:2]))
                        s_mc = mark("S", sS)
                        em.wait("V", sS, s_mc)
                        if tt >= 1:
                            em.wait("V", sT, tile_ttr[tt - 1])
                        em.op("V", lambda e, psl=psl: e.scalar_tensor_tensor(
                            lg_sb[:], rsb_sb[:], cols[:, 6:7], psl,
                            op0=ALU.mult, op1=ALU.subtract))
                        em.op("V", lambda e: e.scalar_tensor_tensor(
                            lgf_sb[:], lg_sb[:], cols[:, 7:8], rbb_sb[:],
                            op0=ALU.mult, op1=ALU.add))
                        v_lgf = mark("V", sV)
                        em.wait("S", sV, v_lgf)
                        em.op("S", lambda e: e.copy(lg_sb[:], lgf_sb[:]))
                        s_lgf = mark("S", sS)
                        em.wait("V", sS, s_lgf)
                        em.op("V", lambda e: e.reduce_max(
                            cols[:, 0:1], lg_sb[:], axis=AX.X, negate=True))
                        em.op("V", lambda e: e.reduce_max(cols[:, 1:2], lg_sb[:], axis=AX.X))
                        v_mx = mark("V", sV)
                        em.wait("S", sV, v_mx)
                        em.op("S", lambda e: e.activation(
                            e_sb[:], lg_sb[:], AF.Exp, bias=cols[:, 0:1],
                            accum_out=cols[:, 3:4]))
                        em.op("S", lambda e: e.copy(cols[:, 2:3], cols[:, 1:2]))
                        s_exp = mark("S", sS)
                        em.wait("V", sS, s_exp)
                        em.op("V", lambda e: e.reciprocal(cols[:, 4:5], cols[:, 3:4]))
                        em.op("V", lambda e: e.tensor_single_scalar(
                            mask_f[:], lg_sb[:], cols[:, 2:3], op=ALU.is_ge))
                        v_msk = mark("V", sV)
                        tile_cons[tt] = v_msk
                        if tt == 0:
                            dump(f"esb_s{s}", e_sb[:])
                            dump(f"cols_s{s}", cols[:])
                            dump(f"lgf_s{s}", lgf_sb[:])
                            dump(f"lg_s{s}", lg_sb[:])
                            dump(f"rsb_s{s}", rsb_sb[:])
                            dump(f"rbb_s{s}", rbb_sb[:])
                        em.wait("T", sV, v_msk)
                        if tt >= 1:
                            em.wait("T", sV, tile_trc[tt - 1])
                        em.op("T", lambda e: e.transpose(
                            ps_mm[6][0:NE, 0:128], mask_f[:], identf_sb[:]))
                        em.op("T", lambda e: e.transpose(
                            ps_mm[7][0:1, 0:128], cols[:, 4:5], identf_sb[:]))
                        t_tr = mark("T", sT)
                        tile_ttr[tt] = t_tr
                        em.wait("V", sT, t_tr)
                        em.op("V", lambda e, tt=tt: e.tensor_copy(
                            maskT_sb[:, tt * 128:(tt + 1) * 128], ps_mm[6][0:NE, 0:128]))
                        em.op("V", lambda e, tt=tt: e.tensor_copy(
                            r_hw[:, tt * 128:(tt + 1) * 128], ps_mm[7][0:1, 0:128]))
                        v_trc = mark("V", sV)
                        tile_trc[tt] = v_trc
                    dump(f"maskT_s{s}", maskT_sb[:])
                    dump(f"rhw_s{s}", r_hw)
                    v_ln = apply_fn()
                    em.wait("T", sG, ld_small_g)
                    em.wait("T", sV, tile_trc[3])
                    for m in (0, 1):
                        em.op("T", lambda e, m=m: e.matmul(
                            ps_mm[m][:], eb_bf[:, m * 128:(m + 1) * 128], maskT_sb[:],
                            start=True, stop=False, skip_group_check=True))
                    em.wait("T", sV, v_ln)
                    for m in (2, 3):
                        em.op("T", lambda e, m=m: e.matmul(
                            ps_mm[m][:], eb_bf[:, m * 128:(m + 1) * 128], maskT_sb[:],
                            start=True, stop=False, skip_group_check=True))
                    bc_guard = [0, 0, 0, 0]
                    mbc_src = [None] * NE
                    for me in range(NE + 1):
                        bank = ps_mm[4 + (me % 4)]
                        if bc_guard[me % 4]:
                            eng, val = bc_guard[me % 4]
                            em.wait("T", sV if eng == "V" else sS, val)
                        if me == 0:
                            em.op("T", lambda e, bank=bank: e.matmul(
                                bank[:], halfr_f[:], r_hw,
                                start=True, stop=True, skip_group_check=True))
                        else:
                            em.op("T", lambda e, me=me, bank=bank: e.matmul(
                                bank[:], esel_sb[:, (me - 1) * 128: me * 128], maskT_sb[:],
                                start=True, stop=True, skip_group_check=True))
                        t_mb = mark("T", sT)
                        ceng = "V" if me % 2 == 0 else "S"
                        em.wait(ceng, sT, t_mb)
                        if me == 0:
                            em.op("V", lambda e, bank=bank: e.tensor_copy(
                                hwbc_sb[:], bank[:]))
                        elif ceng == "V":
                            em.op("V", lambda e, me=me, bank=bank: e.tensor_copy(
                                mbc_sb[:, (me - 1) * SLAB: me * SLAB], bank[:]))
                        else:
                            em.op("S", lambda e, me=me, bank=bank: e.copy(
                                mbc_sb[:, (me - 1) * SLAB: me * SLAB], bank[:]))
                        mk = mark(ceng, sS if ceng == "S" else sV)
                        bc_guard[me % 4] = (ceng, mk)
                        if me >= 1:
                            mbc_src[me - 1] = (ceng, mk)
                    v_mbc = sV.n
                    s_mbc = sS.n
                    em.wait("T", sS, s_mbc)
                    dump(f"mbc_s{s}", mbc_sb[:])
                    dump(f"hwbc_s{s}", hwbc_sb[:])
                    em.wait("T", sV, v_mbc)
                    em.wait("T", sS, s_mbc)
                    for m in (4, 5, 6, 7):
                        em.op("T", lambda e, m=m: e.matmul(
                            ps_mm[m][:], eb_bf[:, m * 128:(m + 1) * 128], maskT_sb[:],
                            start=True, stop=False, skip_group_check=True))
                    t_chunk = [0] * (NE * C)
                    wv_v = [0] * (NE * C)
                    casting = cast and s == 0

                    def w_load(j):
                        par4 = j % 6
                        if (not casting) and j % 2 == 1:
                            return
                        if (not casting) and j % 2 == 0:
                            pr = (j // 2) % 3
                            em.wait("Y", sY, cast_done[min(j + 1, NE * C - 1)])
                            g1 = t_chunk[j - 6] if j >= 6 else wbuf_guard[j % 6]
                            g2 = (t_chunk[j - 5] if j >= 5 else wbuf_guard[(j + 1) % 6])
                            em.wait("Y", sT, max(g1, g2))
                            wv_v[j] = wv_v[j + 1] = y_dma(
                                Wpair[pr][:].rearrange("p (c n) -> p c n", c=2),
                                w_scr.rearrange("(i p) n -> i p n", p=128)
                                [j:j + 2].rearrange("i p n -> p i n"))
                            return
                        if casting and j % 2 == 0:
                            pc2 = (j // 2) % 2
                            em.wait("G", sY, cast_done[j - 3] if j >= 4 else 0)
                            cast_g[j] = g_dma(
                                Wcast[pc2][:].rearrange("p (c n) -> p c n", c=2),
                                ew_d.rearrange("e (c p) n -> (e c) p n", p=128)
                                [j:j + 2].rearrange("i p n -> p i n"))
                            em.wait("Y", sG, cast_g[j])
                            cast_done[j] = cast_done[j + 1] = y_dma(
                                w_scr.rearrange("(i p) n -> i p n", p=128)
                                [j:j + 2].rearrange("i p n -> p i n"),
                                Wcast[pc2][:].rearrange("p (c n) -> p c n", c=2))
                        em.wait("Y", sY, cast_done[j])
                        em.wait("Y", sT, t_chunk[j - 6] if j >= 6 else wbuf_guard[par4])
                        wv_v[j] = y_dma(Wbuf[par4][:], scr_r[j])

                    for j0 in range(5):
                        w_load(j0)
                    for i in range(NE * C):
                        e_i, k_i = divmod(i, C)
                        par = i % 2
                        if i >= 2:
                            em.wait("V", sT, t_chunk[i - 2])
                        if mbc_src[e_i] and mbc_src[e_i][0] == "S":
                            em.wait("V", sS, mbc_src[e_i][1])
                        em.op("V", lambda e, e_i=e_i, k_i=k_i, par=par: e.tensor_mul(
                            xe_buf[par][:],
                            xln_buf[:, k_i * SLAB:(k_i + 1) * SLAB],
                            mbc_sb[:, e_i * SLAB:(e_i + 1) * SLAB]))
                        xe_v = mark("V", sV)
                        if i + 5 < NE * C:
                            w_load(i + 5)
                        em.wait("T", sV, xe_v)
                        em.wait("T", sY, wv_v[i])
                        for m in range(8):
                            em.op("T", lambda e, m=m, par=par, par4=(i % 6),
                                  last=(i == NE * C - 1): e.matmul(
                                ps_mm[m][:], Wbuf[par4][:, m * 128:(m + 1) * 128],
                                xe_buf[par][:],
                                start=False, stop=last,
                                skip_group_check=True))
                        t_chunk[i] = mark("T", sT)
                    wbuf_guard = [t_chunk[max(i2 for i2 in range(NE * C) if i2 % 6 == b)]
                                  for b in range(6)]
                    v_sq_next = None
                    if s + 1 < NSLAB:
                        v_sq_next = emit_ln_sq((s + 1) * SLAB, xbf_buf, sq_buf)
                    em.wait("V", sT, t_chunk[-1])
                    ev_done = {}
                    for m in (0, 1, 4, 5, 2, 3, 6, 7):
                        em.op("V", lambda e, m=m: e.tensor_mul(tmp_buf[:], ps_mm[m][:], hwbc_sb[:]))
                        em.op("V", lambda e, m=m, t0=t0: e.tensor_add(
                            x_chunk(m, t0, SLAB), x_chunk(m, t0, SLAB), tmp_buf[:]))
                        ev_done[m] = mark("V", sV)
                    v_evac = ev_done[7]
                    ffn_bank_free[0] = ev_done
                    em.wait("T", sV, ev_done[1])
                    dump(f"xpost_s{s}", x_res[:])
                    if store:
                        em.wait("G", sV, v_evac)
                        g_dma(yT.rearrange("(c p) t -> p c t", p=128)[:, :, t0:t0 + SLAB],
                              x_res[:].rearrange("p (c t) -> p c t", c=C)[:, :, t0:t0 + SLAB])

        em.wait("V", sY, ld_x)


        if "ffn1" in stages:
            emit_ffn(store=False, cast=True)
            barrier()

        if "attn" in stages:
            ffn_bank_free[0] = None
            with contextlib.ExitStack() as pes:
                pc = pes.enter_context
                psA2 = pc(nc.psum_tensor("l2_a", [128, SLAB], F32))
                psB2 = pc(nc.psum_tensor("l2_b", [128, SLAB], F32))
                ps_s0 = pc(nc.psum_tensor("l2_s0", [1, SLAB], F32))
                ps_s1 = pc(nc.psum_tensor("l2_s1", [1, SLAB], F32))
                ps_p = [pc(nc.psum_tensor(f"pr{i}", [128, 512], F32)) for i in range(4)]
                xbf2 = pc(nc.sbuf_tensor("l2_xbf", [128, C * SLAB], BF16))
                sq2 = xbf2
                tmp2 = pc(nc.sbuf_tensor("l2_tmp", [128, SLAB], F32))
                wq_bf = pc(nc.sbuf_tensor("wq_bf", [128, C * DC], BF16))
                wkv_bf = pc(nc.sbuf_tensor("wkv_bf", [128, C * DC], BF16))
                wqr_bf = pc(nc.sbuf_tensor("wqr_bf", [128, C * DR], BF16))
                wkr_bf = pc(nc.sbuf_tensor("wkr_bf", [128, C * DR], BF16))
                cs_buf = pc(nc.sbuf_tensor("cs_buf", [128, 1024], F32))
                rr_buf = pc(nc.sbuf_tensor("rr_buf", [128, 1024], F32))
                rt_buf = pc(nc.sbuf_tensor("rt_buf", [128, 512], F32))

                gq = g_dma(wq_bf[:].rearrange("p (c n) -> p c n", c=C), r3(wq_d))
                gkv = g_dma(wkv_bf[:].rearrange("p (c n) -> p c n", c=C), r3(wkv_d))
                gqr = g_dma(wqr_bf[:].rearrange("p (c n) -> p c n", c=C), r3(wqr_d))
                gkr = g_dma(wkr_bf[:].rearrange("p (c n) -> p c n", c=C), r3(wkr_d))

                evac_hist = []
                rope_hist = []
                rope_v_hist = []
                for s in range(NSLAB):
                    t0 = s * SLAB
                    n = s
                    v_ln = emit_ln(t0, lambda c, t0=t0: A[:, c * S + t0: c * S + t0 + SLAB],
                                   psA2[:], psB2[:], ps_s0[0:1, :], ps_s1[0:1, :],
                                   xbf2, sq2, tmp2, inplace_sq=True)
                    em.wait("T", sG, gkv)
                    em.wait("T", sV, v_ln)
                    for (wbf, bcol, dst) in ((wq_bf, bq_c, cq_sb), (wkv_bf, bkv_c, ckv_sb)):
                        for m2 in range(2):
                            ps = ps_p[len(evac_hist) % 4]
                            if len(evac_hist) >= 4:
                                em.wait("T", sS, evac_hist[-4])
                            for k in range(C):
                                em.op("T", lambda e, ps=ps, wbf=wbf, m2=m2, n=n, k=k: e.matmul(
                                    ps[:], wbf[:, k * DC + m2 * 128: k * DC + m2 * 128 + 128],
                                    A[:, k * S + n * 512: k * S + n * 512 + 512],
                                    start=(k == 0), stop=(k == C - 1), skip_group_check=True))
                            t_p = mark("T", sT)
                            em.wait("S", sT, t_p)
                            em.op("S", lambda e, ps=ps, dst=dst, bcol=bcol, m2=m2, n=n:
                                  e.activation(
                                      dst[:, m2 * S + n * 512: m2 * S + n * 512 + 512],
                                      ps[:], AF.Gelu, bias=bcol[:, m2:m2 + 1]))
                            evac_hist.append(mark("S", sS))
                    for (wbf, bcol, dstg) in ((wqr_bf, bqr_c, qrg_sb), (wkr_bf, bkr_c, krg_sb)):
                        ps = ps_p[len(evac_hist) % 4]
                        if len(evac_hist) >= 4:
                            em.wait("T", sS, evac_hist[-4])
                        for k in range(C):
                            em.op("T", lambda e, ps=ps, wbf=wbf, n=n, k=k: e.matmul(
                                ps[:], wbf[:, k * DR:(k + 1) * DR],
                                A[:, k * S + n * 512: k * S + n * 512 + 512],
                                start=(k == 0), stop=(k == C - 1), skip_group_check=True))
                        t_p = mark("T", sT)
                        cso = 0
                        if len(rope_v_hist) >= 1:
                            em.wait("Y", sV, rope_v_hist[-1])
                        y_dma(cs_buf[:, cso:cso + 512], cosT_d[:, n * 512:(n + 1) * 512])
                        ys = y_dma(cs_buf[:, cso + 512:cso + 1024],
                                   srot_d[:, n * 512:(n + 1) * 512])
                        em.wait("S", sT, t_p)
                        em.op("S", lambda e, ps=ps, bcol=bcol: e.activation(
                            rr_buf[:, 0:512], ps[:], AF.Identity, bias=bcol[:, 0:1]))
                        s_p = mark("S", sS)
                        evac_hist.append(s_p)
                        em.wait("V", sS, s_p)
                        em.wait("V", sY, ys)
                        em.op("V", lambda e: e.tensor_copy(
                            rt_buf[0:64, :], rr_buf[64:128, 0:512]))
                        em.op("V", lambda e: e.tensor_copy(
                            rt_buf[64:128, :], rr_buf[0:64, 0:512]))
                        em.op("V", lambda e, cso=cso: e.tensor_mul(
                            rt_buf[:], rt_buf[:], cs_buf[:, cso + 512:cso + 1024]))
                        em.op("V", lambda e, cso=cso: e.tensor_mul(
                            rr_buf[:, 512:1024], rr_buf[:, 0:512], cs_buf[:, cso:cso + 512]))
                        em.op("V", lambda e: e.tensor_add(
                            rr_buf[:, 512:1024], rr_buf[:, 512:1024], rt_buf[:]))
                        v_rope = mark("V", sV)
                        rope_v_hist.append(v_rope)
                        em.wait("S", sV, v_rope)
                        em.op("S", lambda e, dstg=dstg, n=n: e.activation(
                            dstg[:, n * 512:(n + 1) * 512], rr_buf[:, 512:1024], AF.Gelu))
                        s_g = mark("S", sS)
                        rope_hist.append(s_g)
                        em.wait("V", sS, s_g)
                barrier()

            with contextlib.ExitStack() as hes:
                hc = hes.enter_context
                ps_sc = [hc(nc.psum_tensor(f"h_sc{i}", [128, 1024], F32)) for i in range(2)]
                ps_out = [hc(nc.psum_tensor(f"h_out{i}", [128, 1024], F32)) for i in range(2)]
                Qbuf = hc(nc.sbuf_tensor("Qbuf", [128, S], BF16))
                Kbuf = hc(nc.sbuf_tensor("Kbuf", [128, S], BF16))
                Vbuf = hc(nc.sbuf_tensor("Vbuf", [128, NT * 256], BF16))
                strips_all = hc(nc.sbuf_tensor("strips_all", [128, 6144], BF16))
                strip = [strips_all[:, i * 1024:(i + 1) * 1024] for i in range(6)]
                sqh = strips_all[:, 0:S]
                rbc_sb = hc(nc.sbuf_tensor("rbc_sb", [128, 1024], BF16))
                rrow_f = rows[:, 0:1024]
                rrow = rows[:].bitcast(BF16)[:, 4096:5120]
                mcol = hc(nc.sbuf_tensor("mcol", [2, 12], F32))

                em.op("V", lambda e: e.memset(Vbuf[:], 0.0))
                em.op("V", lambda e: e.memset(
                    Vbuf[:].rearrange("p (t c) -> p t c", t=NT)[:, :, 64:65], 1.0))
                em.op("V", lambda e: e.memset(
                    Vbuf[:].rearrange("p (t c) -> p t c", t=NT)[:, :, 128:129], 1.0))
                v_ones = mark("V", sV)
                em.wait("T", sV, v_ones)

                pso_guard = [0, 0]

                def build_pair(P):
                    vals = {}
                    em.wait("S", sT, sT.n)
                    em.wait("V", sT, sT.n)
                    em.wait("V", sS, sS.n)
                    em.wait("T", sS, sS.n)
                    em.wait("T", sV, sV.n)
                    if P == 7:
                        qb, kb = qrg_sb, krg_sb
                    else:
                        qb, kb = Qbuf, Kbuf
                        bhist = []
                        for (wbf, src, bcol, dst) in ((wqc_bf, cq_sb, bqc_c, Qbuf),
                                                      (wkc_bf, ckv_sb, bkc_c, Kbuf)):
                            for n in range(4):
                                ps = ps_sc[n % 2][:, 0:512]
                                if len(bhist) >= 2:
                                    em.wait("T", sS, bhist[-2])
                                for p in range(2):
                                    h = 2 * P + p
                                    for kc in range(2):
                                        em.op("T", lambda e, ps=ps, wbf=wbf, h=h, p=p, n=n,
                                              kc=kc, src=src: e.matmul(
                                            ps[64 * p:64 * p + 64, :],
                                            wbf[:, kc * EKR + h * 64: kc * EKR + h * 64 + 64],
                                            src[:, kc * S + n * 512: kc * S + n * 512 + 512],
                                            start=(kc == 0), stop=(kc == 1),
                                            skip_group_check=True))
                                t_q = mark("T", sT)
                                em.wait("S", sT, t_q)
                                em.op("S", lambda e, ps=ps, dst=dst, P=P, n=n,
                                      bcol=bcol: e.activation(
                                    dst[:, n * 512:(n + 1) * 512], ps,
                                    AF.Gelu, bias=bcol[:, P:P + 1]))
                                bhist.append(mark("S", sS))
                        em.wait("T", sS, sS.n)
                    vals["q"] = sS.n
                    em.wait("T", sG, ld_small_g)
                    vhist = []
                    for tt in range(NT):
                        ps = ps_sc[tt % 2][:, 0:128]
                        if len(vhist) >= 2:
                            em.wait("T", sV, vhist[-2])
                        for kc in range(2):
                            em.op("T", lambda e, tt=tt, kc=kc, P=P, ps=ps: e.matmul(
                                ps, ckv_sb[:, kc * S + tt * 128: kc * S + tt * 128 + 128],
                                wv_bf[:, kc * E + P * 128: kc * E + P * 128 + 128],
                                start=(kc == 0), stop=False, skip_group_check=True))
                        em.op("T", lambda e, P=P, ps=ps: e.matmul(
                            ps, onesr_bf[:], bv_row[:, P * 128:(P + 1) * 128],
                            start=False, stop=True, skip_group_check=True))
                        t_v = mark("T", sT)
                        em.wait("V", sT, t_v)
                        em.op("V", lambda e, tt=tt, ps=ps: e.tensor_copy(
                            Vbuf[:, tt * 256: tt * 256 + 64], ps[:, 0:64]))
                        em.op("V", lambda e, tt=tt, ps=ps: e.tensor_copy(
                            Vbuf[:, tt * 256 + 192: tt * 256 + 256], ps[:, 64:128]))
                        vhist.append(mark("V", sV))
                    vals["v"] = sV.n
                    em.wait("T", sV, sV.n)
                    for (buf, col) in ((qb, 0), (kb, 1)):
                        em.op("V", lambda e, buf=buf: e.tensor_mul(sqh[:], buf[:], buf[:]))
                        v_s = mark("V", sV)
                        em.wait("T", sV, v_s)
                        for n in range(4):
                            reg = ps_sc[n // 2][0:2, (n % 2) * 512:(n % 2) * 512 + 512]
                            em.op("T", lambda e, n=n, reg=reg: e.matmul(
                                reg, bo2_sb[:], sqh[:, n * 512:(n + 1) * 512],
                                start=True, stop=True, skip_group_check=True))
                        t_n = mark("T", sT)
                        em.wait("V", sT, t_n)
                        for n in range(4):
                            reg = ps_sc[n // 2][0:2, (n % 2) * 512:(n % 2) * 512 + 512]
                            em.op("V", lambda e, n=n, reg=reg: e.reduce_max(
                                mcol[0:2, 4 + n:5 + n], reg, axis=AX.X))
                        v_parts = mark("V", sV)
                        em.wait("T", sV, v_parts)
                        em.wait("S", sV, v_parts)
                        em.op("S", lambda e: e.copy(mcol[0:2, 8:12], mcol[0:2, 4:8]))
                        s_parts = mark("S", sS)
                        em.wait("V", sS, s_parts)
                        em.op("V", lambda e: e.tensor_max(
                            mcol[0:2, 8:9], mcol[0:2, 8:9], mcol[0:2, 9:10]))
                        em.op("V", lambda e: e.tensor_max(
                            mcol[0:2, 10:11], mcol[0:2, 10:11], mcol[0:2, 11:12]))
                        em.op("V", lambda e, col=col: e.tensor_max(
                            mcol[0:2, col:col + 1], mcol[0:2, 8:9], mcol[0:2, 10:11]))
                    em.op("V", lambda e: e.tensor_add(
                        mcol[0:2, 2:3], mcol[0:2, 0:1], mcol[0:2, 1:2]))
                    em.op("V", lambda e: e.tensor_scalar(
                        mcol[0:2, 3:4], mcol[0:2, 2:3], -0.0625, -1.0,
                        op0=ALU.mult, op1=ALU.add))
                    v_nb = mark("V", sV)
                    em.wait("T", sV, v_nb)
                    em.op("T", lambda e: e.transpose(
                        ps_sc[1][0:1, 0:2], mcol[0:2, 3:4], identf_sb[0:2, 0:2]))
                    t_tr = mark("T", sT)
                    em.wait("V", sT, t_tr)
                    em.op("V", lambda e: e.tensor_copy(nb_row[:], ps_sc[1][0:1, 0:2]))
                    v_nr = mark("V", sV)
                    em.wait("T", sV, v_nr)
                    for p in range(2):
                        em.op("T", lambda e, p=p: e.matmul(
                            ps_sc[1][:, 4 + p:5 + p], onesr_f[:], nb_row[:, p:p + 1],
                            start=True, stop=True, skip_group_check=True))
                    t_bc = mark("T", sT)
                    em.wait("V", sT, t_bc)
                    for p in range(2):
                        em.op("V", lambda e, p=p: e.tensor_copy(
                            negB[p][:], ps_sc[1][:, 4 + p:5 + p]))
                    vals["negB"] = mark("V", sV)
                    em.wait("T", sV, vals["negB"])
                    em.wait("S", sV, vals["negB"])
                    return qb, kb, vals

                for P in range(8):
                    qb, kb, vals = build_pair(P)
                    last_half_se = [{0: 0, 1: 0}, {0: 0, 1: 0}]
                    last_pv = [0, 0]
                    for Hh in range(2):
                        q0, q1 = Hh * 1024, (Hh + 1) * 1024
                        kt_max = q1 // 128
                        t_s = [[{} for _ in range(kt_max)] for _ in range(2)]
                        s_e = [[{} for _ in range(kt_max)] for _ in range(2)]
                        t_pv = [[0] * kt_max, [0] * kt_max]
                        v_m = [[0] * kt_max, [0] * kt_max]
                        started = [[False, False], [False, False]]

                        def espan(kt):
                            ks = kt * 128
                            qs = max(q0, ks)
                            return ks, qs, q1 - qs

                        def halves(kt):
                            ks, qs, span = espan(kt)
                            out = []
                            off = 0
                            while off < span:
                                out.append((off, min(512, span - off)))
                                off += 512
                            return out

                        def eS(p, kt, off, ln):
                            ks, qs, span = espan(kt)
                            h2 = off // 512
                            em.wait("T", sS, last_half_se[p][h2])
                            em.op("T", lambda e, kt=kt, ks=ks, qs=qs, off=off,
                                  ln=ln, p=p: e.matmul(
                                ps_sc[p][:, off:off + ln],
                                kb[64 * p:64 * p + 64, ks:ks + 128],
                                qb[64 * p:64 * p + 64, qs + off: qs + off + ln],
                                start=True, stop=True, skip_group_check=True))
                            t_s[p][kt][h2] = mark("T", sT)

                        def eE(p, kt, off, ln):
                            ks, qs, span = espan(kt)
                            h2 = off // 512
                            em.wait("S", sT, t_s[p][kt][h2])
                            if kt >= 3:
                                em.wait("S", sT, t_pv[p][kt - 3])
                            else:
                                em.wait("S", sT, last_pv[p])
                            em.op("S", lambda e, kt=kt, off=off, ln=ln, p=p: e.activation(
                                strip[3 * p + kt % 3][:, off:off + ln],
                                ps_sc[p][:, off:off + ln],
                                AF.Exp, bias=negB[p][:], scale=0.125))
                            s_e[p][kt][h2] = mark("S", sS)
                            last_half_se[p][h2] = s_e[p][kt][h2]
                            if qs == ks and off == 0:
                                em.wait("V", sS, s_e[p][kt][h2])
                                em.op("V", lambda e, kt=kt, p=p: e.tensor_mul(
                                    strip[3 * p + kt % 3][:, 0:128],
                                    strip[3 * p + kt % 3][:, 0:128], tri_sb[:]))
                                v_m[p][kt] = mark("V", sV)

                        def ePV(p, kt):
                            ks, qs, span = espan(kt)
                            for h2, (off, ln) in enumerate(halves(kt)):
                                em.wait("T", sS, s_e[p][kt][h2])
                            if v_m[p][kt]:
                                em.wait("T", sV, v_m[p][kt])
                            for nb in range(2):
                                lo = max(qs, q0 + nb * 512)
                                hi = q0 + (nb + 1) * 512
                                if lo >= hi:
                                    continue
                                st = not started[p][nb]
                                if st:
                                    em.wait("T", sV, pso_guard[p])
                                em.op("T", lambda e, kt=kt, qs=qs, lo=lo, hi=hi,
                                      p=p, st=st, q0=q0: e.matmul(
                                    ps_out[p][:, lo - q0: hi - q0],
                                    Vbuf[:, kt * 256 + p * 128: kt * 256 + p * 128 + 128],
                                    strip[3 * p + kt % 3][:, lo - qs: hi - qs],
                                    start=st, stop=False, skip_group_check=True))
                                started[p][nb] = True
                            t_pv[p][kt] = mark("T", sT)

                        for kt in range(kt_max):
                            hs = halves(kt)
                            for (off, ln) in hs:
                                eS(0, kt, off, ln)
                                eS(1, kt, off, ln)
                                eE(0, kt, off, ln)
                                eE(1, kt, off, ln)
                            if kt >= 2:
                                ePV(0, kt - 2)
                                ePV(1, kt - 2)
                        for kt in (kt_max - 2, kt_max - 1):
                            ePV(0, kt)
                            ePV(1, kt)
                        last_pv = [t_pv[0][kt_max - 1], t_pv[1][kt_max - 1]]
                        for p in range(2):
                            h = 2 * P + p
                            pso = ps_out[p]
                            drow = 64 if p == 0 else 0
                            brow = 64 if p == 0 else 0
                            data0 = 0 if p == 0 else 64
                            em.wait("S", sT, t_pv[p][kt_max - 1])
                            em.op("S", lambda e, pso=pso, drow=drow: e.activation(
                                rrow_f[:], pso[drow:drow + 1, :], AF.Ln))
                            em.op("S", lambda e: e.activation(
                                rrow[:], rrow_f[:], AF.Exp, scale=-1.0))
                            s_rr = mark("S", sS)
                            em.wait("T", sS, s_rr)
                            for nb2 in range(2):
                                em.op("T", lambda e, nb2=nb2, pso=pso, brow=brow: e.matmul(
                                    pso[brow:brow + 64, nb2 * 512:(nb2 + 1) * 512],
                                    onesr_bf[0:1, 0:64],
                                    rrow[:, nb2 * 512:(nb2 + 1) * 512],
                                    start=True, stop=True, skip_group_check=True))
                            t_rb = mark("T", sT)
                            em.wait("V", sT, t_rb)
                            em.op("V", lambda e, pso=pso, brow=brow, data0=data0:
                                  e.tensor_copy(rbc_sb[data0:data0 + 64, :],
                                                pso[brow:brow + 64, :]))
                            for nb2 in range(2):
                                em.op("V", lambda e, pso=pso, nb2=nb2, h=h, Hh=Hh,
                                      data0=data0: e.tensor_mul(
                                    A[data0:data0 + 64,
                                      (h // 2) * S + Hh * 1024 + nb2 * 512:
                                      (h // 2) * S + Hh * 1024 + nb2 * 512 + 512],
                                    pso[data0:data0 + 64, nb2 * 512:(nb2 + 1) * 512],
                                    rbc_sb[data0:data0 + 64,
                                           nb2 * 512:(nb2 + 1) * 512]))
                            pso_guard[p] = mark("V", sV)
                dump("Aattn", A[:])
                barrier()


            with contextlib.ExitStack() as oes:
                oc = oes.enter_context
                ps_o = [oc(nc.psum_tensor(f"wo{m}", [128, 512], F32)) for m in range(8)]
                wo_buf = [oc(nc.sbuf_tensor(f"wo_b{i}", [128, E], BF16)) for i in range(2)]
                wo_r = wo_d.rearrange("(k p) n -> k p n", p=128)
                wo_read = [0, 0]
                for n in range(4):
                    for k in range(C):
                        par = k % 2
                        em.wait("G", sT, wo_read[par])
                        gv = g_dma(wo_buf[par][:], wo_r[k])
                        em.wait("T", sG, gv)
                        for m in range(8):
                            em.op("T", lambda e, m=m, k=k, n=n, par=par: e.matmul(
                                ps_o[m][:], wo_buf[par][:, m * 128:(m + 1) * 128],
                                A[:, k * S + n * 512: k * S + n * 512 + 512],
                                start=(k == 0), stop=(k == C - 1), skip_group_check=True))
                        wo_read[par] = mark("T", sT)
                    em.wait("V", sT, wo_read[(C - 1) % 2])
                    for m in range(8):
                        em.op("V", lambda e, m=m, n=n: e.scalar_tensor_tensor(
                            x_chunk(m, n * 512, 512), ps_o[m][:], bo_c[:, m:m + 1],
                            x_chunk(m, n * 512, 512), op0=ALU.add, op1=ALU.add))
                    v_n = mark("V", sV)
                    em.wait("T", sV, v_n)
                barrier()

        if "ffn2" in stages:
            emit_ffn(store=True, tag="b")
        else:
            barrier()
            for s in range(NSLAB):
                t0 = s * SLAB
                g_dma(yT.rearrange("(c p) t -> p c t", p=128)[:, :, t0:t0 + SLAB],
                      x_res[:].rearrange("p (c t) -> p c t", c=C)[:, :, t0:t0 + SLAB])

        em.wait("G", sG, sG.n)
        em.wait("G", sY, sY.n)

        block = ec(nc.Block())
        block.tensor(lambda eng: [f(eng) for f in em.q["T"]])
        block.vector(lambda eng: [f(eng) for f in em.q["V"]])
        block.scalar(lambda eng: [f(eng) for f in em.q["S"]])
        block.gpsimd(lambda eng: [f(eng) for f in em.q["G"]])
        block.sync(lambda eng: [f(eng) for f in em.q["Y"]])

    return nc


_CACHE = {}


def _program(stages=_STAGES):
    key = tuple(stages)
    if key not in _CACHE:
        nc = bass.Bass("TRN2", target_bir_lowering=False)
        build(nc, stages=stages)
        _CACHE[key] = nc
    return _CACHE[key]


def _in_maps(inputs):
    g = lambda k: np.asarray(inputs[k], dtype=np.float32)
    shared = dict(
        wq=g("L0_attn_wq"), bq=g("L0_attn_bq").reshape(DC, 1),
        wqc=g("L0_attn_wqc"), bqc=g("L0_attn_bqc").reshape(EKR, 1),
        wqr=g("L0_attn_wqr"), bqr=g("L0_attn_bqr").reshape(DR, 1),
        wkv=g("L0_attn_wkv"), bkv=g("L0_attn_bkv").reshape(DC, 1),
        wkr=g("L0_attn_wkr"), bkr=g("L0_attn_bkr").reshape(DR, 1),
        wkc=g("L0_attn_wkc"), bkc=g("L0_attn_bkc").reshape(EKR, 1),
        wv=g("L0_attn_wv"), bv=g("L0_attn_bv").reshape(1, E),
        wo=g("L0_attn_wo"), bo=g("L0_attn_bo").reshape(E, 1),
        rwT=np.ascontiguousarray(g("L0_ffn_router_w").T),
        rb=g("L0_ffn_router_b").reshape(1, NE),
        ewT=np.ascontiguousarray(g("L0_ffn_expert_w").transpose(0, 2, 1)),
        eb=g("L0_ffn_expert_b"),
    )
    x = g("L0_attn_x")
    maps = []
    for i in range(8):
        m = dict(shared)
        m["xT"] = np.ascontiguousarray(x[i].T)
        maps.append(m)
    return maps


def _run(inputs, stages=_STAGES):
    nc = _program(stages)
    res = run_bass_kernel_spmd(nc, _in_maps(inputs), core_ids=list(range(8)))
    return np.stack([np.asarray(r["yT"], dtype=np.float32).T for r in res.results], axis=0)


def kernel(**inputs):
    return _run(inputs)
